# version 10
# baseline (speedup 1.0000x reference)
import contextlib
import numpy as np
import ml_dtypes
import concourse.bass as bass
import concourse.mybir as mybir
from concourse.bass_utils import run_bass_kernel_spmd

F32 = mybir.dt.float32
F32R = mybir.dt.float32r
BF16 = mybir.dt.bfloat16
AF = mybir.ActivationFunctionType
ALU = mybir.AluOpType
AX = mybir.AxisListType
BFNP = ml_dtypes.bfloat16

D = 1024
NCH = 8
T = 4096
NT = 8
TN = 512
EPS = 1e-5
RSQ_DK = float(1.0 / np.sqrt(128.0))

ENGS = ("te", "sc", "ve", "gp", "sy")
DMA_SEM = {"sy": "dsy", "sc": "dsc", "gp": "dgp", "ve": "dve"}


class Buf:
    __slots__ = ("tok_w", "toks_r", "name")

    def __init__(self, name=""):
        self.tok_w = None
        self.toks_r = []
        self.name = name


class Prog:
    def __init__(self):
        self.stream = {e: [] for e in ENGS}
        self.cnt = {}
        self.waited = {e: {} for e in ENGS}

    def _emit_wait(self, eng, tok):
        if tok is None:
            return
        sem, val = tok
        w = self.waited[eng]
        if w.get(sem, 0) >= val:
            return
        w[sem] = val
        self.stream[eng].append(("wait", sem, val))

    def op(self, eng, build, reads=(), writes=(), deps=(), dma=False, inc=None):
        toks = []
        for b in reads:
            toks.append(b.tok_w)
        for b in writes:
            toks.append(b.tok_w)
            toks.extend(b.toks_r)
        toks.extend(deps)
        for t in toks:
            self._emit_wait(eng, t)
        if dma:
            sem = DMA_SEM[eng]
            step = 16 if inc is None else inc
        else:
            sem = eng
            step = 1 if inc is None else inc
        if step:
            after = self.cnt.get(sem, 0) + step
            self.cnt[sem] = after
            tok = (sem, after)
        else:
            tok = None
        self.stream[eng].append(("inst", build, sem if step else None, step))
        for b in reads:
            if tok is not None:
                b.toks_r.append(tok)
        for b in writes:
            if tok is not None:
                b.tok_w = tok
                b.toks_r = []
        return tok

    def finish(self):
        for t in [(s, v) for s, v in self.cnt.items()]:
            self._emit_wait("sy", t)

    def phase_barrier(self):
        toks = [(s, v) for s, v in self.cnt.items()]
        for e in ENGS:
            for t in toks:
                self._emit_wait(e, t)

    def flush(self, nc, sems):
        if not any(self.stream[e] for e in ENGS):
            return
        with nc.Block() as block:
            for ename, attr in (("te", "tensor"), ("sc", "scalar"), ("ve", "vector"),
                                ("gp", "gpsimd"), ("sy", "sync")):
                items = self.stream[ename]
                if not items:
                    continue

                def make(items):
                    def body(eng):
                        for it in items:
                            if it[0] == "wait":
                                eng.wait_ge(sems[it[1]], it[2])
                            else:
                                _, build, sem, step = it
                                if getattr(build, "needs_sems", False):
                                    build(eng, sems)
                                else:
                                    inst = build(eng)
                                    if sem is not None:
                                        inst.then_inc(sems[sem], step)
                    return body

                getattr(block, attr)(make(items))
        self.stream = {e: [] for e in ENGS}

    def realize(self, nc):
        semnames = sorted(self.cnt.keys())
        with contextlib.ExitStack() as st:
            sems = {s: st.enter_context(nc.semaphore(s)) for s in semnames}
            self.flush(nc, sems)


class Ctx:
    def __init__(self, nc, stack):
        self.nc = nc
        self.p = Prog()
        self.stack = stack

    _uid = 0

    def _u(self, name):
        Ctx._uid += 1
        return f"{name}_{Ctx._uid}"

    def sbuf(self, name, shape, dtype):
        t = self.stack.enter_context(self.nc.sbuf_tensor(self._u(name), shape, dtype))
        return t, Buf(name)

    def psum(self, name, shape, dtype):
        t = self.stack.enter_context(self.nc.psum_tensor(self._u(name), shape, dtype))
        return t, Buf(name)


def ap3(t, stride, nch, col0, ncol):
    row = t.ap().ap[0][0]
    return bass.AP(t, col0, [[row, 128], [stride, nch], [1, ncol]])


def bc3(t, col0, ncol, nch):
    row = t.ap().ap[0][0]
    return bass.AP(t, col0, [[row, 128], [0, nch], [1, ncol]])


def bci(t, col0, nb):
    row = t.ap().ap[0][0]
    return bass.AP(t, col0, [[row, 128], [1, nb], [0, 64]])


def load_weight(ctx, name, key, din, dout, dtype=BF16):
    nc, p = ctx.nc, ctx.p
    dr = nc.dram_tensor(key, [din, dout], F32, kind="ExternalInput")
    sb, buf = ctx.sbuf("sb_" + name, [128, (din // 128) * dout], dtype)
    src = bass.AP(dr, 0, [[dout, 128], [128 * dout, din // 128], [1, dout]])
    dst = ap3(sb, dout, din // 128, 0, dout)
    eng = "gp" if dtype == BF16 else "sy"
    p.op(eng, lambda e, d=dst, s=src: e.dma_start(out=d, in_=s), writes=[buf], dma=True)
    return sb, buf


def wblk(sb, ic, oc, dout):
    return sb[:, ic * dout + oc * 128: ic * dout + (oc + 1) * 128]


def load_vec(ctx, name, key, n):
    nc, p = ctx.nc, ctx.p
    dr = nc.dram_tensor(key, [128, n], F32, kind="ExternalInput")
    sb, buf = ctx.sbuf("sb_" + name, [128, n], F32)
    p.op("sy", lambda e, d=sb, s=dr: e.dma_start(out=d[:, :], in_=s[:, :]), writes=[buf], dma=True)
    return sb, buf


def r_dram(dr, col0, ncol):
    return bass.AP(dr, col0, [[T, 128], [128 * T, NCH], [1, ncol]])


class LN:
    _n = 0

    def __init__(self, ctx, nbuf=2):
        c = ctx
        self.ctx = ctx
        self.nbuf = nbuf
        LN._n += 1
        pf = f"ln{LN._n}_"
        self.rbf = [c.sbuf(pf + f"rbf{i}", [128, NCH * TN], BF16) for i in range(nbuf)]
        self.rsq = [c.sbuf(pf + f"rsq{i}", [128, NCH * TN], BF16) for i in range(nbuf)]
        _scr = c.sbuf(pf + "scr", [128, NCH * TN], F32)
        self.scr = [_scr for i in range(nbuf)]
        self.m2 = [c.sbuf(pf + f"m2{i}", [128, TN], F32) for i in range(nbuf)]
        self.ones, self.b_ones = c.sbuf(pf + "ones", [128, 128], BF16)
        self.psS = [c.psum(pf + f"psS{i}", [128, TN], F32) for i in range(nbuf)]
        self.psQ = [c.psum(pf + f"psQ{i}", [128, TN], F32) for i in range(nbuf)]
        ctx.p.op("gp", lambda e: e.memset(self.ones[:, :], 1.0), writes=[self.b_ones])

    def stats(self, R, b_R, rstride, rcol, n, negmu, b_negmu, mucol, a, b_a, acol, par=0):
        p = self.ctx.p
        par = par % self.nbuf
        rbf, b_rbf = self.rbf[par]
        rsq, b_rsq = self.rsq[par]
        m2, b_m2 = self.m2[par]
        psS, b_psS = self.psS[par]
        psQ, b_psQ = self.psQ[par]
        src = ap3(R, rstride, NCH, rcol, n)
        p.op("sc", lambda e: e.activation(ap3(rsq, TN, NCH, 0, n), src, AF.Square),
             reads=[b_R], writes=[b_rsq])
        p.op("ve", lambda e: e.tensor_copy(ap3(rbf, TN, NCH, 0, n), src),
             reads=[b_R], writes=[b_rbf])
        for ic in range(NCH):
            p.op("te", lambda e, ic=ic: e.matmul(
                psS[:, 0:n], self.ones[:, :], rbf[:, ic * TN: ic * TN + n],
                start=(ic == 0), stop=(ic == NCH - 1)),
                reads=[self.b_ones, b_rbf], writes=[b_psS],
                inc=(1 if ic == NCH - 1 else 0))
        for ic in range(NCH):
            p.op("te", lambda e, ic=ic: e.matmul(
                psQ[:, 0:n], self.ones[:, :], rsq[:, ic * TN: ic * TN + n],
                start=(ic == 0), stop=(ic == NCH - 1)),
                reads=[self.b_ones, b_rsq], writes=[b_psQ],
                inc=(1 if ic == NCH - 1 else 0))
        p.op("sc", lambda e: e.activation(negmu[:, mucol:mucol + n], psS[:, 0:n],
                                          AF.Copy, scale=-1.0 / D),
             reads=[b_psS], writes=[b_negmu])
        p.op("sc", lambda e: e.activation(m2[:, 0:n], negmu[:, mucol:mucol + n], AF.Square),
             reads=[b_negmu], writes=[b_m2])
        p.op("ve", lambda e: e.scalar_tensor_tensor(
            a[:, acol:acol + n], psQ[:, 0:n], 1.0 / D, m2[:, 0:n],
            ALU.mult, ALU.subtract),
            reads=[b_psQ, b_m2], writes=[b_a])
        p.op("ve", lambda e: e.tensor_scalar(
            a[:, acol:acol + n], a[:, acol:acol + n], float(EPS), None, ALU.add),
            reads=[b_a], writes=[b_a])
        p.op("sc", lambda e: e.activation(a[:, acol:acol + n], a[:, acol:acol + n], AF.Ln),
             reads=[b_a], writes=[b_a])
        p.op("sc", lambda e: e.activation(a[:, acol:acol + n], a[:, acol:acol + n], AF.Exp,
                                          scale=-0.5),
             reads=[b_a], writes=[b_a])

    def apply(self, R, b_R, rstride, rcol, n, negmu, b_negmu, mucol, a, b_a, acol,
              xn, b_xn, xstride, xcol, par=0):
        p = self.ctx.p
        par = par % self.nbuf
        scr, b_scr = self.scr[par]
        src = ap3(R, rstride, NCH, rcol, n)
        p.op("ve", lambda e: e.tensor_tensor(
            ap3(scr, TN, NCH, 0, n), src, bc3(negmu, mucol, n, NCH), ALU.add),
            reads=[b_R, b_negmu], writes=[b_scr])
        p.op("ve", lambda e: e.tensor_tensor(
            ap3(xn, xstride, NCH, xcol, n), ap3(scr, TN, NCH, 0, n),
            bc3(a, acol, n, NCH), ALU.mult),
            reads=[b_scr, b_a], writes=[b_xn])

    def apply_chunked(self, R, b_R, rstride, rcol, n, negmu, b_negmu, mucol,
                      a, b_a, acol, xn, b_xn, xstride, xcol, scr, b_scr):
        p = self.ctx.p
        rrow = R.ap().ap[0][0]
        xrow = xn.ap().ap[0][0]
        for ic in range(NCH):
            rsl = bass.AP(R, ic * rstride + rcol, [[rrow, 128], [1, n]])
            xsl = bass.AP(xn, ic * xstride + xcol, [[xrow, 128], [1, n]])
            p.op("ve", lambda e, rsl=rsl: e.tensor_tensor(
                scr[:, 0:n], rsl, negmu[:, mucol:mucol + n], ALU.add),
                reads=[b_R, b_negmu], writes=[b_scr])
            p.op("ve", lambda e, xsl=xsl: e.tensor_tensor(
                xsl, scr[:, 0:n], a[:, acol:acol + n], ALU.mult),
                reads=[b_scr, b_a], writes=[b_xn])


def emit_proj(ctx, wsb, b_w, dout, xn, b_xn, xstride, xcol, n, pools, evac):
    p = ctx.p
    noc = dout // 128
    xrow = xn.ap().ap[0][0]
    xbufs = b_xn if isinstance(b_xn, list) else [b_xn] * NCH
    for oc in range(noc):
        ps, b_ps = pools[oc % len(pools)]
        for ic in range(NCH):
            rhs = bass.AP(xn, ic * xstride + xcol, [[xrow, 128], [1, n]])
            p.op("te", lambda e, ic=ic, oc=oc, ps=ps, rhs=rhs: e.matmul(
                ps[:, 0:n], wblk(wsb, ic, oc, dout), rhs,
                start=(ic == 0), stop=(ic == NCH - 1)),
                reads=[b_w, xbufs[ic]], writes=[b_ps],
                inc=(1 if ic == NCH - 1 else 0))
        evac(oc, ps, b_ps)



def build_m0():
    nc = bass.Bass("TRN2", target_bir_lowering=False)
    with contextlib.ExitStack() as st:
        ctx = Ctx(nc, st)
        p = ctx.p
        wrT, b_wrT = load_weight(ctx, "wrT", "wrT", D, D, F32)
        wiT, b_wiT = load_weight(ctx, "wiT", "wiT", D, D, F32)
        fr, b_fr = load_weight(ctx, "fr", "fr", D, D, F32)
        nfi, b_nfi = load_weight(ctx, "nfi", "nfi", D, D, F32)
        grs, b_grs = load_weight(ctx, "grs", "grs", D, 128, F32)
        ngis, b_ngis = load_weight(ctx, "ngis", "ngis", D, 128, F32)
        gis, b_gis = load_weight(ctx, "gis", "gis", D, 128, F32)
        m0d = nc.dram_tensor("m0s", [D, 128], F32, kind="ExternalOutput")

        ar, b_ar = ctx.sbuf("ar", [128, NCH * 128], F32)
        ai, b_ai = ctx.sbuf("ai", [128, NCH * 128], F32)
        mo, b_mo = ctx.sbuf("mo", [128, NCH * 128], F32)
        ps, b_ps = ctx.psum("m0ps", [128, 128], F32)

        def stage(pairs, dst, b_dst):
            for oc in range(NCH):
                total = len(pairs) * NCH
                k = 0
                for (lsb, b_l, ldout, rsb, b_r) in pairs:
                    for ic in range(NCH):
                        k += 1
                        is_last = (k == total)
                        p.op("te", lambda e, lsb=lsb, ldout=ldout, rsb=rsb, ic=ic, oc=oc,
                             s=(k == 1), t=is_last: e.matmul(
                                 ps[:, :], wblk(lsb, ic, oc, ldout),
                                 rsb[:, ic * 128:(ic + 1) * 128], start=s, stop=t),
                             reads=[b_l, b_r], writes=[b_ps], inc=(1 if is_last else 0))
                p.op("sc", lambda e, oc=oc, dst=dst: e.copy(dst[:, oc * 128:(oc + 1) * 128], ps[:, :]),
                     reads=[b_ps], writes=[b_dst])

        stage([(wrT, b_wrT, D, grs, b_grs), (wiT, b_wiT, D, ngis, b_ngis)], ar, b_ar)
        stage([(wrT, b_wrT, D, gis, b_gis), (wiT, b_wiT, D, grs, b_grs)], ai, b_ai)
        stage([(fr, b_fr, D, ar, b_ar), (nfi, b_nfi, D, ai, b_ai)], mo, b_mo)

        dst = bass.AP(m0d, 0, [[128, 128], [128 * 128, NCH], [1, 128]])
        p.op("sy", lambda e: e.dma_start(out=dst, in_=ap3(mo, 128, NCH, 0, 128)),
             reads=[b_mo], dma=True)
        p.finish()
        p.realize(nc)
    return nc


def build_ffn0():
    nc = bass.Bass("TRN2", target_bir_lowering=False)
    with contextlib.ExitStack() as st:
        ctx = Ctx(nc, st)
        p = ctx.p
        rin = nc.dram_tensor("r", [D, T], F32, kind="ExternalInput")
        rout = nc.dram_tensor("rout", [D, T], F32, kind="ExternalOutput")
        wm0, b_wm0 = load_weight(ctx, "wm0", "wm0", D, D, BF16)
        bl, b_bl = load_vec(ctx, "bias_l", "bias_l", NCH)

        ln = LN(ctx)
        Rt, b_Rt = ctx.sbuf("Rt", [128, NCH * TN], F32)
        xn, b_xn = ctx.sbuf("xn", [128, NCH * TN], BF16)
        negmu, b_negmu = ctx.sbuf("negmu", [128, TN], F32)
        av, b_av = ctx.sbuf("aval", [128, TN], F32)
        ev, b_ev = ctx.sbuf("ev", [128, TN], F32)
        pools = [ctx.psum(f"pp{i}", [128, TN], F32) for i in range(2)]

        for j in range(NT):
            col = j * TN
            p.op("sy", lambda e, col=col: e.dma_start(
                out=ap3(Rt, TN, NCH, 0, TN), in_=r_dram(rin, col, TN)),
                writes=[b_Rt], dma=True)
            ln.stats(Rt, b_Rt, TN, 0, TN, negmu, b_negmu, 0, av, b_av, 0)
            ln.apply(Rt, b_Rt, TN, 0, TN, negmu, b_negmu, 0, av, b_av, 0, xn, b_xn, TN, 0)

            def evac(oc, ps, b_ps):
                p.op("ve", lambda e, oc=oc, ps=ps: e.tensor_scalar(
                    ev[:, :], ps[:, 0:TN], bl[:, oc:oc + 1], 0.5, ALU.add, ALU.mult),
                    reads=[b_ps, b_bl], writes=[b_ev])
                p.op("ve", lambda e, oc=oc: e.tensor_tensor(
                    Rt[:, oc * TN:(oc + 1) * TN], Rt[:, oc * TN:(oc + 1) * TN],
                    ev[:, :], ALU.add),
                    reads=[b_ev], writes=[b_Rt])

            emit_proj(ctx, wm0, b_wm0, D, xn, b_xn, TN, 0, TN, pools, evac)
            p.op("sy", lambda e, col=col: e.dma_start(
                out=r_dram(rout, col, TN), in_=ap3(Rt, TN, NCH, 0, TN)),
                reads=[b_Rt], dma=True)
        p.finish()
        p.realize(nc)
    return nc


def build_attn0():
    nc = bass.Bass("TRN2", target_bir_lowering=False)
    with contextlib.ExitStack() as st:
        ctx = Ctx(nc, st)
        p = ctx.p
        rin = nc.dram_tensor("r", [D, T], F32, kind="ExternalInput")
        rout = nc.dram_tensor("rout", [D, T], F32, kind="ExternalOutput")
        wqkv, b_wqkv = load_weight(ctx, "wqkv", "wqkvT", D, 3 * D, BF16)
        wout, b_wout = load_weight(ctx, "wout", "woutT", D, D, BF16)
        bq, b_bq = load_vec(ctx, "bqkv_l", "bqkv_l", 3 * NCH)
        bo, b_bo = load_vec(ctx, "bout_l", "bout_l", NCH)
        idt, b_idt = load_weight(ctx, "ident", "ident", 128, 128, BF16)

        ln = LN(ctx)
        Rt, b_Rt = ctx.sbuf("Rt", [128, NCH * TN], F32)
        xn, b_xn = ctx.sbuf("xn", [128, NCH * TN], BF16)
        negmu, b_negmu = ctx.sbuf("negmu", [128, TN], F32)
        avl, b_avl = ctx.sbuf("aval", [128, TN], F32)
        qkv, b_qkv = ctx.sbuf("qkv", [128, 3 * NCH * TN], BF16)
        ssub, b_ssub = ctx.sbuf("ssub", [128, TN], F32)
        ebf, b_ebf = ctx.sbuf("ebf", [128, TN], BF16)
        abf, b_abf = ctx.sbuf("abf", [128, TN], BF16)
        mx, b_mx = ctx.sbuf("mx", [128, NT], F32)
        sm, b_sm = ctx.sbuf("sm", [128, NT], F32)
        rs, b_rs = ctx.sbuf("rsum", [128, NT], F32)
        atT, b_atT = ctx.sbuf("atT", [64, 1024], BF16)
        vT, b_vT = ctx.sbuf("vT", [64, 1024], BF16)
        avo, b_avo = ctx.sbuf("avo", [128, NCH * TN], BF16)
        ev, b_ev = ctx.sbuf("ev", [128, TN], F32)
        pools = [ctx.psum(f"pp{i}", [128, TN], F32) for i in range(2)]
        psS, b_psS = ctx.psum("att_psS", [128, TN], F32)
        psA, b_psA = ctx.psum("att_psA", [128, TN], F32)
        psT1, b_psT1 = ctx.psum("att_psT1", [64, 1024], BF16)
        psT2, b_psT2 = ctx.psum("att_psT2", [64, 1024], BF16)

        for j in range(NT):
            col = j * TN
            p.op("sy", lambda e, col=col: e.dma_start(
                out=ap3(Rt, TN, NCH, 0, TN), in_=r_dram(rin, col, TN)),
                writes=[b_Rt], dma=True)
            ln.stats(Rt, b_Rt, TN, 0, TN, negmu, b_negmu, 0, avl, b_avl, 0)
            ln.apply(Rt, b_Rt, TN, 0, TN, negmu, b_negmu, 0, avl, b_avl, 0, xn, b_xn, TN, 0)

            def evq(oc, ps, b_ps):
                sc = 0.125 if oc < NCH else 1.0
                p.op("ve", lambda e, oc=oc, ps=ps, sc=sc: e.tensor_scalar(
                    qkv[:, oc * TN:(oc + 1) * TN], ps[:, 0:TN],
                    bq[:, oc:oc + 1], sc, ALU.add, ALU.mult),
                    reads=[b_ps, b_bq], writes=[b_qkv])

            emit_proj(ctx, wqkv, b_wqkv, 3 * D, xn, b_xn, TN, 0, TN, pools, evq)

            for cch in range(NCH):
                qc = cch * TN
                kc = (NCH + cch) * TN
                vc = (2 * NCH + cch) * TN
                for blk in range(8):
                    o = blk * 64
                    p.op("te", lambda e, qc=qc, kc=kc, o=o: e.matmul(
                        psS[0:64, o:o + 64], qkv[0:64, qc + o:qc + o + 64],
                        qkv[0:64, kc + o:kc + o + 64],
                        start=True, stop=True, tile_position=(0, 0)),
                        reads=[b_qkv], writes=[b_psS])
                    p.op("te", lambda e, qc=qc, kc=kc, o=o: e.matmul(
                        psS[64:128, o:o + 64], qkv[64:128, qc + o:qc + o + 64],
                        qkv[64:128, kc + o:kc + o + 64],
                        start=True, stop=True, tile_position=(64, 64)),
                        reads=[b_qkv], writes=[b_psS])
                ps3 = psS.ap().rearrange("p (b n) -> p b n", n=64)
                p.op("ve", lambda e, ps3=ps3: e.tensor_reduce(mx[:, :], ps3, AX.X, ALU.max),
                     reads=[b_psS], writes=[b_mx])
                p.op("ve", lambda e, ps3=ps3: e.tensor_tensor(
                    ssub.ap().rearrange("p (b n) -> p b n", n=64), ps3, bci(mx, 0, 8),
                    ALU.subtract),
                    reads=[b_psS, b_mx], writes=[b_ssub])
                p.op("sc", lambda e: e.activation(ebf[:, :], ssub[:, :], AF.Exp),
                     reads=[b_ssub], writes=[b_ebf])
                p.op("ve", lambda e: e.tensor_reduce(
                    sm[:, :], ebf.ap().rearrange("p (b n) -> p b n", n=64), AX.X, ALU.add),
                    reads=[b_ebf], writes=[b_sm])
                p.op("ve", lambda e: e.reciprocal(rs[:, :], sm[:, :]), reads=[b_sm], writes=[b_rs])
                p.op("ve", lambda e: e.tensor_tensor(
                    abf.ap().rearrange("p (b n) -> p b n", n=64),
                    ebf.ap().rearrange("p (b n) -> p b n", n=64), bci(rs, 0, 8), ALU.mult),
                    reads=[b_ebf, b_rs], writes=[b_abf])
                for blk in range(8):
                    o = blk * 64
                    p.op("te", lambda e, o=o, blk=blk: e.transpose(
                        psT1[:, blk * 128:(blk + 1) * 128], abf[:, o:o + 64], idt[:, :]),
                        reads=[b_abf, b_idt], writes=[b_psT1])
                    p.op("te", lambda e, o=o, blk=blk, vc=vc: e.transpose(
                        psT2[:, blk * 128:(blk + 1) * 128], qkv[:, vc + o:vc + o + 64], idt[:, :]),
                        reads=[b_qkv, b_idt], writes=[b_psT2])
                p.op("sc", lambda e: e.copy(atT[:, :], psT1[:, :]), reads=[b_psT1], writes=[b_atT])
                p.op("sc", lambda e: e.copy(vT[:, :], psT2[:, :]), reads=[b_psT2], writes=[b_vT])
                for blk in range(8):
                    o = blk * 64
                    p.op("te", lambda e, blk=blk, o=o: e.matmul(
                        psA[0:64, o:o + 64], vT[0:64, blk * 128:blk * 128 + 64],
                        atT[0:64, blk * 128:blk * 128 + 64],
                        start=True, stop=True, tile_position=(0, 0)),
                        reads=[b_vT, b_atT], writes=[b_psA])
                    p.op("te", lambda e, blk=blk, o=o: e.matmul(
                        psA[64:128, o:o + 64], vT[0:64, blk * 128 + 64:blk * 128 + 128],
                        atT[0:64, blk * 128 + 64:blk * 128 + 128],
                        start=True, stop=True, tile_position=(0, 64)),
                        reads=[b_vT, b_atT], writes=[b_psA])
                p.op("sc", lambda e, cch=cch: e.copy(avo[:, cch * TN:(cch + 1) * TN], psA[:, :]),
                     reads=[b_psA], writes=[b_avo])

            def evo(oc, ps, b_ps):
                p.op("ve", lambda e, oc=oc, ps=ps: e.tensor_scalar(
                    ev[:, :], ps[:, 0:TN], bo[:, oc:oc + 1], None, ALU.add),
                    reads=[b_ps, b_bo], writes=[b_ev])
                p.op("ve", lambda e, oc=oc: e.tensor_tensor(
                    Rt[:, oc * TN:(oc + 1) * TN], Rt[:, oc * TN:(oc + 1) * TN],
                    ev[:, :], ALU.add),
                    reads=[b_ev], writes=[b_Rt])

            emit_proj(ctx, wout, b_wout, D, avo, b_avo, TN, 0, TN, pools, evo)
            p.op("sy", lambda e, col=col: e.dma_start(
                out=r_dram(rout, col, TN), in_=ap3(Rt, TN, NCH, 0, TN)),
                reads=[b_Rt], dma=True)
        p.finish()
        p.realize(nc)
    return nc


def build_ffn1():
    nc = bass.Bass("TRN2", target_bir_lowering=False)
    with contextlib.ExitStack() as st:
        ctx = Ctx(nc, st)
        p = ctx.p
        rin = nc.dram_tensor("r", [D, T], F32, kind="ExternalInput")
        rout = nc.dram_tensor("rout", [D, T], F32, kind="ExternalOutput")
        halo = nc.dram_tensor("halo", [128, NCH], F32, kind="ExternalInput")
        wr, b_wr = load_weight(ctx, "wr1", "wrT1", D, D, BF16)
        wk, b_wk = load_weight(ctx, "wk1", "wkT1", D, D, BF16)
        wv, b_wv = load_weight(ctx, "wv1", "wvT1", D, D, BF16)
        mur, b_mur = load_vec(ctx, "mur_l", "mur_l", NCH)
        muk, b_muk = load_vec(ctx, "muk_l", "muk_l", NCH)

        ln = LN(ctx)
        Rt, b_Rt = ctx.sbuf("Rt", [128, NCH * TN], F32)
        negmu, b_negmu = ctx.sbuf("negmu", [128, T], F32)
        avl, b_avl = ctx.sbuf("aval", [128, T], F32)
        hmu, b_hmu = ctx.sbuf("hmu", [128, 1], F32)
        ha, b_ha = ctx.sbuf("ha", [128, 1], F32)
        ht, b_ht = ctx.sbuf("ht", [128, NCH], F32)
        hxn, b_hxn = ctx.sbuf("hxn", [128, NCH], BF16)
        XW = TN + 1
        xe = [ctx.sbuf(f"xe{i}", [128, NCH * XW], BF16) for i in range(2)]
        dt, b_dt = ctx.sbuf("dt", [128, NCH * TN], BF16)
        gx, b_gx = ctx.sbuf("gx", [128, NCH * TN], BF16)
        xr, b_xr = ctx.sbuf("xr", [128, NCH * TN], BF16)
        xk, b_xk = ctx.sbuf("xk", [128, NCH * TN], BF16)
        rg, b_rg = ctx.sbuf("rg", [128, NCH * TN], F32)
        krel, b_krel = ctx.sbuf("krel", [128, TN], F32)
        kact, b_kact = ctx.sbuf("kact", [128, NCH * TN], BF16)
        tv, b_tv = ctx.sbuf("tv", [128, TN], F32)
        pools = [ctx.psum(f"pp{i}", [128, TN], F32) for i in range(2)]

        p.op("sy", lambda e: e.dma_start(out=ht[:, :], in_=halo[:, :]),
             writes=[b_ht], dma=True)

        for j in range(NT):
            col = j * TN
            p.op("sy", lambda e, col=col: e.dma_start(
                out=ap3(Rt, TN, NCH, 0, TN), in_=r_dram(rin, col, TN)),
                writes=[b_Rt], dma=True)
            ln.stats(Rt, b_Rt, TN, 0, TN, negmu, b_negmu, col, avl, b_avl, col)
        ln.stats(ht, b_ht, 1, 0, 1, hmu, b_hmu, 0, ha, b_ha, 0)
        ln.apply(ht, b_ht, 1, 0, 1, hmu, b_hmu, 0, ha, b_ha, 0, hxn, b_hxn, 1, 0)

        for j in range(NT):
            col = j * TN
            xei, b_xei = xe[j % 2]
            xprev, b_xprev = xe[(j + 1) % 2]
            p.op("sy", lambda e, col=col: e.dma_start(
                out=ap3(Rt, TN, NCH, 0, TN), in_=r_dram(rin, col, TN)),
                writes=[b_Rt], dma=True)
            if j == 0:
                p.op("ve", lambda e, xei=xei: e.tensor_copy(
                    ap3(xei, XW, NCH, 0, 1), ap3(hxn, 1, NCH, 0, 1)),
                    reads=[b_hxn], writes=[b_xei])
            else:
                p.op("ve", lambda e, xei=xei, xprev=xprev: e.tensor_copy(
                    ap3(xei, XW, NCH, 0, 1), ap3(xprev, XW, NCH, TN, 1)),
                    reads=[b_xprev], writes=[b_xei])
            ln.apply(Rt, b_Rt, TN, 0, TN, negmu, b_negmu, col, avl, b_avl, col,
                     xei, b_xei, XW, 1)
            cur = ap3(xei, XW, NCH, 1, TN)
            prv = ap3(xei, XW, NCH, 0, TN)
            p.op("ve", lambda e, cur=cur, prv=prv: e.tensor_tensor(
                ap3(dt, TN, NCH, 0, TN), cur, prv, ALU.subtract),
                reads=[b_xei], writes=[b_dt])
            for (mu, b_mu, xdst, b_xdst) in ((mur, b_mur, xr, b_xr), (muk, b_muk, xk, b_xk)):
                for ic in range(NCH):
                    p.op("ve", lambda e, ic=ic, mu=mu: e.tensor_scalar(
                        gx[:, ic * TN:(ic + 1) * TN], dt[:, ic * TN:(ic + 1) * TN],
                        mu[:, ic:ic + 1], None, ALU.mult),
                        reads=[b_dt, b_mu], writes=[b_gx])
                p.op("ve", lambda e, xdst=xdst, prv=prv: e.tensor_tensor(
                    ap3(xdst, TN, NCH, 0, TN), ap3(gx, TN, NCH, 0, TN), prv, ALU.add),
                    reads=[b_gx, b_xei], writes=[b_xdst])

            def ev_r(oc, ps, b_ps):
                p.op("sc", lambda e, oc=oc, ps=ps: e.activation(
                    rg[:, oc * TN:(oc + 1) * TN], ps[:, 0:TN], AF.Sigmoid),
                    reads=[b_ps], writes=[b_rg])

            emit_proj(ctx, wr, b_wr, D, xr, b_xr, TN, 0, TN, pools, ev_r)

            def ev_k(oc, ps, b_ps):
                p.op("ve", lambda e, ps=ps: e.tensor_scalar(
                    krel[:, :], ps[:, 0:TN], 0.0, None, ALU.max),
                    reads=[b_ps], writes=[b_krel])
                p.op("sc", lambda e, oc=oc: e.activation(
                    kact[:, oc * TN:(oc + 1) * TN], krel[:, :], AF.Square),
                    reads=[b_krel], writes=[b_kact])

            emit_proj(ctx, wk, b_wk, D, xk, b_xk, TN, 0, TN, pools, ev_k)

            def ev_v(oc, ps, b_ps):
                p.op("ve", lambda e, oc=oc, ps=ps: e.tensor_tensor(
                    tv[:, :], ps[:, 0:TN], rg[:, oc * TN:(oc + 1) * TN], ALU.mult),
                    reads=[b_ps, b_rg], writes=[b_tv])
                p.op("ve", lambda e, oc=oc: e.scalar_tensor_tensor(
                    Rt[:, oc * TN:(oc + 1) * TN], tv[:, :], 0.5,
                    Rt[:, oc * TN:(oc + 1) * TN], ALU.mult, ALU.add),
                    reads=[b_tv], writes=[b_Rt])

            emit_proj(ctx, wv, b_wv, D, kact, b_kact, TN, 0, TN, pools, ev_v)
            p.op("sy", lambda e, col=col: e.dma_start(
                out=r_dram(rout, col, TN), in_=ap3(Rt, TN, NCH, 0, TN)),
                reads=[b_Rt], dma=True)
        p.finish()
        p.realize(nc)
    return nc


def build_linattn_m():
    nc = bass.Bass("TRN2", target_bir_lowering=False)
    with contextlib.ExitStack() as st:
        ctx = Ctx(nc, st)
        p = ctx.p
        rin = nc.dram_tensor("r", [D, T], F32, kind="ExternalInput")
        mout = nc.dram_tensor("mpart", [128, D], F32, kind="ExternalOutput")
        wk, b_wk = load_weight(ctx, "wk", "wkT", D, 128, BF16)
        idt, b_idt = load_weight(ctx, "ident", "ident", 128, 128, BF16)

        ln = LN(ctx)
        Rt, b_Rt = ctx.sbuf("Rt", [128, NCH * TN], F32)
        xn, b_xn = ctx.sbuf("xn", [128, NCH * TN], BF16)
        negmu, b_negmu = ctx.sbuf("negmu", [128, TN], F32)
        avl, b_avl = ctx.sbuf("aval", [128, TN], F32)
        ktok, b_ktok = ctx.sbuf("ktok", [128, 128], BF16)
        jtok, b_jtok = ctx.sbuf("jtok", [128, 1024], BF16)
        msb, b_msb = ctx.sbuf("msb", [128, 1024], F32)
        psK, b_psK = ctx.psum("la_psK", [128, 128], F32)
        psJ, b_psJ = ctx.psum("la_psJ", [128, 1024], BF16)
        psM = [ctx.psum(f"la_psM{i}", [128, TN], F32) for i in range(2)]

        nblk = T // 128
        for j in range(NT):
            col = j * TN
            p.op("sy", lambda e, col=col: e.dma_start(
                out=ap3(Rt, TN, NCH, 0, TN), in_=r_dram(rin, col, TN)),
                writes=[b_Rt], dma=True)
            ln.stats(Rt, b_Rt, TN, 0, TN, negmu, b_negmu, 0, avl, b_avl, 0)
            ln.apply(Rt, b_Rt, TN, 0, TN, negmu, b_negmu, 0, avl, b_avl, 0, xn, b_xn, TN, 0)
            for tb in range(TN // 128):
                gi = j * (TN // 128) + tb
                for ic in range(NCH):
                    p.op("te", lambda e, ic=ic, tb=tb: e.matmul(
                        psK[:, :], xn[:, ic * TN + tb * 128: ic * TN + (tb + 1) * 128],
                        wk[:, ic * 128:(ic + 1) * 128],
                        start=(ic == 0), stop=(ic == NCH - 1)),
                        reads=[b_xn, b_wk], writes=[b_psK],
                        inc=(1 if ic == NCH - 1 else 0))
                p.op("sc", lambda e: e.copy(ktok[:, :], psK[:, :]), reads=[b_psK], writes=[b_ktok])
                for ic in range(NCH):
                    p.op("te", lambda e, ic=ic, tb=tb: e.transpose(
                        psJ[:, ic * 128:(ic + 1) * 128],
                        xn[:, ic * TN + tb * 128: ic * TN + (tb + 1) * 128], idt[:, :]),
                        reads=[b_xn, b_idt], writes=[b_psJ])
                p.op("sc", lambda e: e.copy(jtok[:, :], psJ[:, :]), reads=[b_psJ], writes=[b_jtok])
                for half in range(2):
                    ps, b_ps = psM[half]
                    p.op("te", lambda e, ps=ps, half=half, s=(gi == 0), t=(gi == nblk - 1): e.matmul(
                        ps[:, :], ktok[:, :], jtok[:, half * TN:(half + 1) * TN],
                        start=s, stop=t),
                        reads=[b_ktok, b_jtok], writes=[b_ps])
        for half in range(2):
            ps, b_ps = psM[half]
            p.op("sc", lambda e, half=half, ps=ps: e.copy(
                msb[:, half * TN:(half + 1) * TN], ps[:, :]),
                reads=[b_ps], writes=[b_msb])
        p.op("sy", lambda e: e.dma_start(out=mout[:, :], in_=msb[:, :]),
             reads=[b_msb], dma=True)
        p.finish()
        p.realize(nc)
    return nc


def build_linattn_apply():
    nc = bass.Bass("TRN2", target_bir_lowering=False)
    with contextlib.ExitStack() as st:
        ctx = Ctx(nc, st)
        p = ctx.p
        rin = nc.dram_tensor("r", [D, T], F32, kind="ExternalInput")
        rout = nc.dram_tensor("rout", [D, T], F32, kind="ExternalOutput")
        mp = nc.dram_tensor("mpair", [2 * 128, D], F32, kind="ExternalInput")
        wq, b_wq = load_weight(ctx, "wq", "wqT", D, 128, BF16)
        gam, b_gam = load_vec(ctx, "gam_l", "gam_l", NCH)
        bet, b_bet = load_vec(ctx, "bet_l", "bet_l", NCH)

        m0sb, b_m0sb = ctx.sbuf("m0sb", [128, D], F32)
        m1sb, b_m1sb = ctx.sbuf("m1sb", [128, D], F32)
        mbf, b_mbf = ctx.sbuf("mbf", [128, D], BF16)
        p.op("sy", lambda e: e.dma_start(out=m0sb[:, :], in_=mp[0:128, :]),
             writes=[b_m0sb], dma=True)
        p.op("sy", lambda e: e.dma_start(out=m1sb[:, :], in_=mp[128:256, :]),
             writes=[b_m1sb], dma=True)
        p.op("ve", lambda e: e.tensor_tensor(mbf[:, :], m0sb[:, :], m1sb[:, :], ALU.add),
             reads=[b_m0sb, b_m1sb], writes=[b_mbf])

        ln = LN(ctx)
        Rt, b_Rt = ctx.sbuf("Rt", [128, NCH * TN], F32)
        xn, b_xn = ctx.sbuf("xn", [128, NCH * TN], BF16)
        negmu, b_negmu = ctx.sbuf("negmu", [128, TN], F32)
        avl, b_avl = ctx.sbuf("aval", [128, TN], F32)
        qbf, b_qbf = ctx.sbuf("qbf", [128, TN], BF16)
        jn, b_jn = ctx.sbuf("jn", [128, NCH * TN], F32)
        nmu2, b_nmu2 = ctx.sbuf("nmu2", [128, TN], F32)
        av2, b_av2 = ctx.sbuf("av2", [128, TN], F32)
        g1, b_g1 = ctx.sbuf("g1", [128, TN], F32)
        pools = [ctx.psum(f"pp{i}", [128, TN], F32) for i in range(2)]

        for j in range(NT):
            col = j * TN
            p.op("sy", lambda e, col=col: e.dma_start(
                out=ap3(Rt, TN, NCH, 0, TN), in_=r_dram(rin, col, TN)),
                writes=[b_Rt], dma=True)
            ln.stats(Rt, b_Rt, TN, 0, TN, negmu, b_negmu, 0, avl, b_avl, 0)
            ln.apply(Rt, b_Rt, TN, 0, TN, negmu, b_negmu, 0, avl, b_avl, 0, xn, b_xn, TN, 0)

            def evq(oc, ps, b_ps):
                p.op("sc", lambda e, ps=ps: e.copy(qbf[:, :], ps[:, 0:TN]),
                     reads=[b_ps], writes=[b_qbf])

            emit_proj(ctx, wq, b_wq, 128, xn, b_xn, TN, 0, TN, pools, evq)

            for dc in range(NCH):
                ps, b_ps = pools[dc % 2]
                p.op("te", lambda e, dc=dc, ps=ps: e.matmul(
                    ps[:, 0:TN], mbf[:, dc * 128:(dc + 1) * 128], qbf[:, :],
                    start=True, stop=True),
                    reads=[b_mbf, b_qbf], writes=[b_ps])
                p.op("ve", lambda e, dc=dc, ps=ps: e.scalar_tensor_tensor(
                    jn[:, dc * TN:(dc + 1) * TN], ps[:, 0:TN], RSQ_DK,
                    xn[:, dc * TN:(dc + 1) * TN], ALU.mult, ALU.add),
                    reads=[b_ps, b_xn], writes=[b_jn])

            ln.stats(jn, b_jn, TN, 0, TN, nmu2, b_nmu2, 0, av2, b_av2, 0)
            p.op("ve", lambda e: e.tensor_tensor(
                ap3(ln.scr, TN, NCH, 0, TN), ap3(jn, TN, NCH, 0, TN),
                bc3(nmu2, 0, TN, NCH), ALU.add),
                reads=[b_jn, b_nmu2], writes=[ln.b_scr])
            p.op("ve", lambda e: e.tensor_tensor(
                ap3(ln.scr, TN, NCH, 0, TN), ap3(ln.scr, TN, NCH, 0, TN),
                bc3(av2, 0, TN, NCH), ALU.mult),
                reads=[b_av2], writes=[ln.b_scr])
            for ic in range(NCH):
                p.op("ve", lambda e, ic=ic: e.tensor_scalar(
                    g1[:, :], ln.scr[:, ic * TN:(ic + 1) * TN],
                    gam[:, ic:ic + 1], bet[:, ic:ic + 1], ALU.mult, ALU.add),
                    reads=[ln.b_scr, b_gam, b_bet], writes=[b_g1])
                p.op("ve", lambda e, ic=ic: e.tensor_tensor(
                    Rt[:, ic * TN:(ic + 1) * TN], Rt[:, ic * TN:(ic + 1) * TN],
                    g1[:, :], ALU.add),
                    reads=[b_g1], writes=[b_Rt])
            p.op("sy", lambda e, col=col: e.dma_start(
                out=r_dram(rout, col, TN), in_=ap3(Rt, TN, NCH, 0, TN)),
                reads=[b_Rt], dma=True)
        p.finish()
        p.realize(nc)
    return nc



def _dft_constants():
    dd = np.arange(D, dtype=np.float64)
    ang = -2.0 * np.pi * np.outer(dd, dd) / D
    Fr = np.cos(ang).astype(np.float32)
    Fi = np.sin(ang).astype(np.float32)
    Gr = (np.cos(ang) / D).astype(np.float32)
    Gi = (-np.sin(ang) / D).astype(np.float32)
    return Fr, Fi, Gr, Gi


def _vec_l(v):
    n = v.shape[0] // 128
    return np.ascontiguousarray(v.reshape(n, 128).T.astype(np.float32))


def kernel(**inputs):
    return kernel_fused(**inputs)


def kernel_unfused(L0_attn_x, L0_attn_W_qkv, L0_attn_b_qkv, L0_attn_W_out, L0_attn_b_out,
           L0_ffn_w_freq_real, L0_ffn_w_freq_imag, L0_ffn_bias,
           L1_attn_W_Q, L1_attn_W_K, L1_attn_gamma, L1_attn_beta,
           L1_ffn_W_r, L1_ffn_W_k, L1_ffn_W_v, L1_ffn_mu_r, L1_ffn_mu_k):
    NCORES = 8
    cores = list(range(NCORES))
    x = np.asarray(L0_attn_x, np.float32)

    shards = []
    for c in cores:
        b, h = c // 2, c % 2
        shards.append(np.ascontiguousarray(x[b, h * T:(h + 1) * T, :].T))

    Fr, Fi, Gr, Gi = _dft_constants()
    wrT = np.ascontiguousarray(np.asarray(L0_ffn_w_freq_real, np.float32).T)
    wiT = np.ascontiguousarray(np.asarray(L0_ffn_w_freq_imag, np.float32).T)
    nfi = np.ascontiguousarray(-Fi)

    ncA = build_m0()
    in_maps = []
    for c in cores:
        sl = slice(c * 128, (c + 1) * 128)
        in_maps.append({
            "wrT": wrT, "wiT": wiT, "fr": Fr, "nfi": nfi,
            "grs": np.ascontiguousarray(Gr[:, sl]),
            "ngis": np.ascontiguousarray(-Gi[:, sl]),
            "gis": np.ascontiguousarray(Gi[:, sl]),
        })
    resA = run_bass_kernel_spmd(ncA, in_maps, core_ids=cores)
    m0 = np.concatenate([resA.results[c]["m0s"] for c in cores], axis=1)

    ident = np.eye(128, dtype=np.float32)

    ncB = build_ffn0()
    bias_l = _vec_l(np.asarray(L0_ffn_bias, np.float32))
    resB = run_bass_kernel_spmd(
        ncB, [{"r": shards[c], "wm0": m0, "bias_l": bias_l} for c in cores], core_ids=cores)
    r1 = [resB.results[c]["rout"] for c in cores]

    ncC = build_attn0()
    wqkvT = np.ascontiguousarray(np.asarray(L0_attn_W_qkv, np.float32).T)
    woutT = np.ascontiguousarray(np.asarray(L0_attn_W_out, np.float32).T)
    bqkv_l = _vec_l(np.asarray(L0_attn_b_qkv, np.float32))
    bout_l = _vec_l(np.asarray(L0_attn_b_out, np.float32))
    resC = run_bass_kernel_spmd(
        ncC, [{"r": r1[c], "wqkvT": wqkvT, "bqkv_l": bqkv_l, "woutT": woutT,
               "bout_l": bout_l, "ident": ident} for c in cores], core_ids=cores)
    r2 = [resC.results[c]["rout"] for c in cores]

    resD = run_bass_kernel_spmd(
        ncB, [{"r": r2[c], "wm0": m0, "bias_l": bias_l} for c in cores], core_ids=cores)
    r3 = [resD.results[c]["rout"] for c in cores]

    ncE = build_ffn1()
    wrT1 = np.ascontiguousarray(np.asarray(L1_ffn_W_r, np.float32).T)
    wkT1 = np.ascontiguousarray(np.asarray(L1_ffn_W_k, np.float32).T)
    wvT1 = np.ascontiguousarray(np.asarray(L1_ffn_W_v, np.float32).T)
    mur_l = _vec_l(np.asarray(L1_ffn_mu_r, np.float32))
    muk_l = _vec_l(np.asarray(L1_ffn_mu_k, np.float32))

    def halos(rlist):
        out = []
        for c in cores:
            if c % 2 == 0:
                out.append(np.zeros((128, NCH), np.float32))
            else:
                out.append(_vec_l(rlist[c - 1][:, T - 1]))
        return out

    h3 = halos(r3)
    resE = run_bass_kernel_spmd(
        ncE, [{"r": r3[c], "wrT1": wrT1, "wkT1": wkT1, "wvT1": wvT1,
               "mur_l": mur_l, "muk_l": muk_l, "halo": h3[c]} for c in cores],
        core_ids=cores)
    r4 = [resE.results[c]["rout"] for c in cores]

    ncF = build_linattn_m()
    wkT = np.ascontiguousarray(np.asarray(L1_attn_W_K, np.float32).T)
    resF = run_bass_kernel_spmd(
        ncF, [{"r": r4[c], "wkT": wkT, "ident": ident} for c in cores], core_ids=cores)
    mparts = [resF.results[c]["mpart"] for c in cores]

    ncG = build_linattn_apply()
    wqT = np.ascontiguousarray(np.asarray(L1_attn_W_Q, np.float32).T)
    gam_l = _vec_l(np.asarray(L1_attn_gamma, np.float32))
    bet_l = _vec_l(np.asarray(L1_attn_beta, np.float32))
    resG = run_bass_kernel_spmd(
        ncG, [{"r": r4[c],
               "mpair": np.concatenate([mparts[(c // 2) * 2], mparts[(c // 2) * 2 + 1]], axis=0),
               "wqT": wqT, "gam_l": gam_l, "bet_l": bet_l} for c in cores],
        core_ids=cores)
    r5 = [resG.results[c]["rout"] for c in cores]

    h5 = halos(r5)
    resH = run_bass_kernel_spmd(
        ncE, [{"r": r5[c], "wrT1": wrT1, "wkT1": wkT1, "wvT1": wvT1,
               "mur_l": mur_l, "muk_l": muk_l, "halo": h5[c]} for c in cores],
        core_ids=cores)
    r6 = [resH.results[c]["rout"] for c in cores]

    out = np.empty_like(x)
    for c in cores:
        b, h = c // 2, c % 2
        out[b, h * T:(h + 1) * T, :] = r6[c].T
    return out



MAGIC = 777


class PairComm:

    def __init__(self, ctx, st):
        nc, p = ctx.nc, ctx.p
        self.ctx = ctx
        self.fl = nc.dram_tensor("flags", [2, 8], mybir.dt.int32, addr_space="Shared")
        self.regs = {r: st.enter_context(nc.gpsimd.register("rg_" + r))
                     for r in ("ms", "fv", "cond")}
        self.mrow, self.b_mrow = ctx.sbuf("cc_mrow", [1, 1], mybir.dt.int32)
        self.zrow, self.b_zrow = ctx.sbuf("cc_zrow", [1, 1], mybir.dt.int32)
        self.cid = nc.dram_tensor("cid", [1, 1], mybir.dt.int32, kind="ExternalInput")
        self.cids, self.b_cids = ctx.sbuf("cc_cids", [1, 1], mybir.dt.int32)
        p.op("gp", lambda e: e.memset(self.mrow[:, :], MAGIC), writes=[self.b_mrow])
        p.op("gp", lambda e: e.memset(self.zrow[:, :], 0), writes=[self.b_zrow])
        p.op("gp", lambda e: e.dma_start(out=self.cids[:, :], in_=self.cid[:, :]),
             writes=[self.b_cids], dma=True)
        p.op("gp", lambda e: e.reg_load(self.regs["ms"], self.cids[:1, :1]),
             reads=[self.b_cids])

    def send(self, data_emit, b_datas, idx):
        p = self.ctx.p
        r = self.regs

        after_data = self.ctx.p.cnt.get("dgp", 0) + 16

        def body(e, sems, after_data=after_data):
            with e.If_eq(r["ms"], 0):
                data_emit(e, 0).then_inc(sems["dgp"], 16)
            with e.Else():
                data_emit(e, 1).then_inc(sems["dgp"], 16)
            e.wait_ge(sems["dgp"], after_data)
            with e.If_eq(r["ms"], 0):
                e.dma_start(out=self.fl[0:1, idx:idx + 1],
                            in_=self.mrow[:, :]).then_inc(sems["dgp"], 16)
            with e.Else():
                e.dma_start(out=self.fl[1:2, idx:idx + 1],
                            in_=self.mrow[:, :]).then_inc(sems["dgp"], 16)
        body.needs_sems = True
        tok = p.op("gp", body, reads=list(b_datas) + [self.b_mrow], dma=True, inc=32)
        return tok

    def wait_partner(self, idx):
        p = self.ctx.p
        r = self.regs

        def body(e, sems):
            with e.If_eq(r["ms"], 0):
                e.reg_mov(r["cond"], 1)
                with e.While(r["cond"]):
                    e.load(r["fv"], self.fl[1:2, idx:idx + 1])
                    e.reg_alu(r["cond"], r["fv"], MAGIC, ALU.not_equal)
                e.dma_start(out=self.fl[1:2, idx:idx + 1],
                            in_=self.zrow[:, :]).then_inc(sems["dgp"], 16)
            with e.Else():
                e.reg_mov(r["cond"], 1)
                with e.While(r["cond"]):
                    e.load(r["fv"], self.fl[0:1, idx:idx + 1])
                    e.reg_alu(r["cond"], r["fv"], MAGIC, ALU.not_equal)
                e.dma_start(out=self.fl[0:1, idx:idx + 1],
                            in_=self.zrow[:, :]).then_inc(sems["dgp"], 16)
        body.needs_sems = True
        return p.op("gp", body, reads=[self.b_zrow], dma=True, inc=16)


def fload_weight(ctx, name, dr, din, dout, dtype=BF16):
    p = ctx.p
    sb, buf = ctx.sbuf(name, [128, (din // 128) * dout], dtype)
    src = bass.AP(dr, 0, [[dout, 128], [128 * dout, din // 128], [1, dout]])
    dst = ap3(sb, dout, din // 128, 0, dout)
    eng = "gp" if dtype != F32 else "sy"
    p.op(eng, lambda e, d=dst, s=src: e.dma_start(out=d, in_=s), writes=[buf], dma=True)
    return sb, buf


def fload_vec(ctx, name, dr, n):
    p = ctx.p
    sb, buf = ctx.sbuf(name, [128, n], F32)
    p.op("sy", lambda e, d=sb, s=dr: e.dma_start(out=d[:, :], in_=s[:, :]),
         writes=[buf], dma=True)
    return sb, buf


def emit_m0_phase(ctx, dr, comm):
    nc, p = ctx.nc, ctx.p
    HN = 512
    wrT, b_wrT = fload_weight(ctx, "f_wrT", dr["wrT"], D, D, F32R)
    wiT, b_wiT = fload_weight(ctx, "f_wiT", dr["wiT"], D, D, F32R)
    grs, b_grs = fload_weight(ctx, "f_grs", dr["grh"], D, HN, F32R)
    ngis, b_ngis = fload_weight(ctx, "f_ngis", dr["ngih"], D, HN, F32R)
    gis, b_gis = fload_weight(ctx, "f_gis", dr["gih"], D, HN, F32R)
    mo, b_mo = ctx.sbuf("f_mo", [128, NCH * HN], F32)
    arr, b_arr = ctx.sbuf("f_arr", [128, NCH * HN], F32R)
    air, b_air = ctx.sbuf("f_air", [128, NCH * HN], F32R)
    ps, b_ps = ctx.psum("f_m0ps", [128, HN], F32)
    fring = [(ctx.sbuf(f"f_fr{i}", [128, NCH * 128], F32R),
              ctx.sbuf(f"f_nfi{i}", [128, NCH * 128], F32R)) for i in range(2)]

    def stage1(dst, b_dst, r1, b_r1, r2, b_r2):
        for oc in range(NCH):
            k = 0
            for (lsb, b_l, rsb, b_r) in ((wrT, b_wrT, r1, b_r1), (wiT, b_wiT, r2, b_r2)):
                for ic in range(NCH):
                    k += 1
                    last = (k == 16)
                    p.op("te", lambda e, lsb=lsb, rsb=rsb, ic=ic, oc=oc,
                         s=(k == 1), t=last: e.matmul(
                             ps[:, :], wblk(lsb, ic, oc, D),
                             rsb[:, ic * HN:(ic + 1) * HN], start=s, stop=t),
                         reads=[b_l, b_r], writes=[b_ps], inc=(1 if last else 0))
            p.op("sc", lambda e, oc=oc, dst=dst: e.copy(
                dst[:, oc * HN:(oc + 1) * HN], ps[:, :]),
                reads=[b_ps], writes=[b_dst])

    stage1(arr, b_arr, grs, b_grs, ngis, b_ngis)
    stage1(air, b_air, gis, b_gis, grs, b_grs)

    for oc in range(NCH):
        (frs, b_frs), (nfis, b_nfis) = fring[oc % 2]
        src_fr = bass.AP(dr["fr"], oc * 128, [[D, 128], [128 * D, NCH], [1, 128]])
        src_nfi = bass.AP(dr["nfi"], oc * 128, [[D, 128], [128 * D, NCH], [1, 128]])
        p.op("gp", lambda e, s=src_fr, d=frs: e.dma_start(
            out=ap3(d, 128, NCH, 0, 128), in_=s), writes=[b_frs], dma=True)
        p.op("gp", lambda e, s=src_nfi, d=nfis: e.dma_start(
            out=ap3(d, 128, NCH, 0, 128), in_=s), writes=[b_nfis], dma=True)
        k = 0
        for (lsb, b_l, rsb, b_r) in ((frs, b_frs, arr, b_arr), (nfis, b_nfis, air, b_air)):
            for mc in range(NCH):
                k += 1
                last = (k == 16)
                p.op("te", lambda e, lsb=lsb, rsb=rsb, mc=mc, s=(k == 1), t=last: e.matmul(
                    ps[:, :], lsb[:, mc * 128:(mc + 1) * 128],
                    rsb[:, mc * HN:(mc + 1) * HN], start=s, stop=t),
                    reads=[b_l, b_r], writes=[b_ps],
                    inc=(1 if (k == 8 or k == 16) else 0))
        p.op("sc", lambda e, oc=oc: e.copy(mo[:, oc * HN:(oc + 1) * HN], ps[:, :]),
             reads=[b_ps], writes=[b_mo])

    def wr_m0(e, slot):
        dst = bass.AP(dr["m0sh"], slot * HN, [[D, 128], [128 * D, NCH], [1, HN]])
        return e.dma_start(out=dst, in_=ap3(mo, HN, NCH, 0, HN))
    comm.send(wr_m0, [b_mo], 0)
    comm.wait_partner(0)


def emit_ffn0_phase(ctx, dr, rsrc, rdst):
    p = ctx.p
    wm0, b_wm0 = fload_weight(ctx, "p_wm0", dr["m0sh"], D, D, BF16)
    bl, b_bl = fload_vec(ctx, "p_bias", dr["bias_l"], NCH)
    ln = LN(ctx)
    Rt = [ctx.sbuf(f"p_Rt{i}", [128, NCH * TN], F32) for i in range(2)]
    xnb = [ctx.sbuf(f"p_xn{i}", [128, NCH * TN], BF16) for i in range(2)]
    nmu = [ctx.sbuf(f"p_negmu{i}", [128, TN], F32) for i in range(2)]
    avv = [ctx.sbuf(f"p_aval{i}", [128, TN], F32) for i in range(2)]
    evb = [ctx.sbuf(f"p_ev{i}", [128, TN], F32) for i in range(2)]
    pools = [ctx.psum(f"p_pp{i}", [128, TN], F32) for i in range(4)]

    def frontend(j):
        par = j % 2
        col = j * TN
        R, b_R = Rt[par]
        xn, b_xn = xnb[par]
        negmu, b_negmu = nmu[par]
        av, b_av = avv[par]
        p.op("sy", lambda e, col=col, R=R: e.dma_start(
            out=ap3(R, TN, NCH, 0, TN), in_=r_dram(rsrc, col, TN)),
            writes=[b_R], dma=True)
        ln.stats(R, b_R, TN, 0, TN, negmu, b_negmu, 0, av, b_av, 0, par=par)
        ln.apply(R, b_R, TN, 0, TN, negmu, b_negmu, 0, av, b_av, 0, xn, b_xn, TN, 0, par=par)

    def backend(j):
        par = j % 2
        col = j * TN
        R, b_R = Rt[par]
        xn, b_xn = xnb[par]
        ev, b_ev = evb[par]

        def evac(oc, psb, b_ps):
            p.op("ve", lambda e, oc=oc, psb=psb: e.tensor_scalar(
                ev[:, :], psb[:, 0:TN], bl[:, oc:oc + 1], 0.5, ALU.add, ALU.mult),
                reads=[b_ps, b_bl], writes=[b_ev])
            p.op("ve", lambda e, oc=oc: e.tensor_tensor(
                R[:, oc * TN:(oc + 1) * TN], R[:, oc * TN:(oc + 1) * TN],
                ev[:, :], ALU.add),
                reads=[b_ev], writes=[b_R])

        emit_proj(ctx, wm0, b_wm0, D, xn, b_xn, TN, 0, TN,
                  pools[par * 2:par * 2 + 2], evac)
        p.op("sy", lambda e, col=col, R=R: e.dma_start(
            out=r_dram(rdst, col, TN), in_=ap3(R, TN, NCH, 0, TN)),
            reads=[b_R], dma=True)

    frontend(0)
    for j in range(NT):
        if j + 1 < NT:
            frontend(j + 1)
        backend(j)


def emit_attn0_phase(ctx, dr, rsrc, rdst):
    p = ctx.p
    wqk, b_wqk = ctx.sbuf("p_wqk", [128, NCH * 2048], BF16)
    src = bass.AP(dr["wqkvT"], 0, [[3 * D, 128], [128 * 3 * D, NCH], [1, 2048]])
    p.op("gp", lambda e, src=src: e.dma_start(out=ap3(wqk, 2048, NCH, 0, 2048), in_=src),
         writes=[b_wqk], dma=True)
    wvr, b_wvr = ctx.sbuf("p_wvr", [128, NCH * 1024], BF16)
    srcv = bass.AP(dr["wqkvT"], 2048, [[3 * D, 128], [128 * 3 * D, NCH], [1, 1024]])
    p.op("gp", lambda e, srcv=srcv: e.dma_start(out=ap3(wvr, 1024, NCH, 0, 1024), in_=srcv),
         writes=[b_wvr], dma=True)
    wout, b_wout = fload_weight(ctx, "p_wout", dr["woutT"], D, D, BF16)
    bq, b_bq = fload_vec(ctx, "p_bqkv", dr["bqkv_l"], 3 * NCH)
    bo, b_bo = fload_vec(ctx, "p_bout", dr["bout_l"], NCH)
    idt, b_idt = ctx.sbuf("p_idtb", [128, 128], BF16)
    p.op("gp", lambda e: e.dma_start(out=idt[:, :], in_=dr["ident"][:, :]),
         writes=[b_idt], dma=True)
    vbb, b_vbb = ctx.sbuf("p_vbrow", [1, 1024], BF16)
    p.op("gp", lambda e: e.dma_start(out=vbb[:, :], in_=dr["bv_row"][:, :]),
         writes=[b_vbb], dma=True)

    ln = LN(ctx, nbuf=1)
    Rt = [ctx.sbuf(f"p_Rt{i}", [128, NCH * TN], F32) for i in range(2)]
    xnb = [ctx.sbuf(f"p_xn{i}", [128, NCH * TN], BF16) for i in range(2)]
    nmu = [ctx.sbuf("p_negmu", [128, TN], F32)] * 2
    avv = [ctx.sbuf("p_aval", [128, TN], F32)] * 2
    ascr, b_ascr = ctx.sbuf("p_ascr", [128, TN], F32)
    qkv, _bq0 = ctx.sbuf("p_qkv", [128, 16 * TN], BF16)
    b_qkvs = [Buf(f"qkv{i}") for i in range(16)]
    vt, b_vt = ctx.sbuf("p_vt", [64, 8 * 1024], BF16)
    seb = [ctx.sbuf(f"p_sE{i}", [128, TN], BF16) for i in range(2)]
    abf = [ctx.sbuf(f"p_abf{i}", [128, TN], BF16) for i in range(2)]
    atT = [ctx.sbuf(f"p_atT{i}", [64, 1024], BF16) for i in range(2)]
    mxb = [ctx.sbuf(f"p_mx{i}", [128, NT], F32) for i in range(2)]
    smb = [ctx.sbuf(f"p_sm{i}", [128, NT], F32) for i in range(2)]
    rsb = [ctx.sbuf(f"p_rsum{i}", [128, NT], F32) for i in range(2)]
    avo, _bav0 = ctx.sbuf("p_avo", [128, NCH * TN], BF16)
    b_avos = [Buf(f"avo{i}") for i in range(NCH)]
    pool = [ctx.psum(f"p_pl{i}", [128, TN], F32) for i in range(2)]
    psS, b_psS = ctx.psum("p_psS", [128, TN], F32)
    psA, b_psA = ctx.psum("p_psA", [128, TN], F32)
    psT = [ctx.psum(f"p_psT{i}", [64, 1024], BF16) for i in range(2)]

    def frontend(j):
        par = j % 2
        col = j * TN
        R, b_R = Rt[par]
        xn, b_xn = xnb[par]
        negmu, b_negmu = nmu[par]
        av, b_av = avv[par]
        p.op("sy", lambda e, col=col, R=R: e.dma_start(
            out=ap3(R, TN, NCH, 0, TN), in_=r_dram(rsrc, col, TN)),
            writes=[b_R], dma=True)
        ln.stats(R, b_R, TN, 0, TN, negmu, b_negmu, 0, av, b_av, 0)
        ln.apply_chunked(R, b_R, TN, 0, TN, negmu, b_negmu, 0, av, b_av, 0,
                         xn, b_xn, TN, 0, ascr, b_ascr)

    def backend(j):
        par = j % 2
        col = j * TN
        R, b_R = Rt[par]
        xn, b_xn = xnb[par]

        for tb in range(4):
            for half in range(2):
                psb, b_psb = pool[half]
                for ic in range(NCH):
                    p.op("te", lambda e, tb=tb, half=half, ic=ic, psb=psb, xn=xn: e.matmul(
                        psb[:, 0:TN], xn[:, ic * TN + tb * 128: ic * TN + (tb + 1) * 128],
                        wvr[:, ic * 1024 + half * TN: ic * 1024 + (half + 1) * TN],
                        start=(ic == 0), stop=False),
                        reads=[b_xn, b_wvr], writes=[b_psb], inc=0)
                p.op("te", lambda e, half=half, psb=psb: e.matmul(
                    psb[:, 0:TN], ln.ones[0:1, :], vbb[0:1, half * TN:(half + 1) * TN],
                    start=False, stop=True),
                    reads=[ln.b_ones, b_vbb], writes=[b_psb])
                for sub in range(2):
                    blk = tb * 2 + sub
                    p.op("sc", lambda e, blk=blk, sub=sub, half=half, psb=psb: e.copy(
                        vt[0:64, blk * 1024 + half * TN: blk * 1024 + (half + 1) * TN],
                        psb[sub * 64:(sub + 1) * 64, 0:TN]),
                        reads=[b_psb], writes=[b_vt])

        def qk(c):
            for qq in (c, NCH + c):
                psb, b_psb = pool[qq % 2]
                for ic in range(NCH):
                    p.op("te", lambda e, qq=qq, ic=ic, psb=psb, xn=xn: e.matmul(
                        psb[:, 0:TN], wblk(wqk, ic, qq, 2 * D),
                        bass.AP(xn, ic * TN, [[xn.ap().ap[0][0], 128], [1, TN]]),
                        start=(ic == 0), stop=(ic == NCH - 1)),
                        reads=[b_wqk, b_xn], writes=[b_psb],
                        inc=(1 if ic == NCH - 1 else 0))
                sc = 0.125 if qq < NCH else 1.0
                p.op("ve", lambda e, qq=qq, psb=psb, sc=sc: e.tensor_scalar(
                    qkv[:, qq * TN:(qq + 1) * TN], psb[:, 0:TN],
                    bq[:, qq:qq + 1], sc, ALU.add, ALU.mult),
                    reads=[b_psb, b_bq], writes=[b_qkvs[qq]])

        def scores(c):
            qc = c * TN
            kc = (NCH + c) * TN
            for blk in range(8):
                o = blk * 64
                p.op("te", lambda e, qc=qc, kc=kc, o=o: e.matmul(
                    psS[0:64, o:o + 64], qkv[0:64, qc + o:qc + o + 64],
                    qkv[0:64, kc + o:kc + o + 64],
                    start=True, stop=True, tile_position=(0, 0)),
                    reads=[b_qkvs[c], b_qkvs[NCH + c]], writes=[b_psS])
                p.op("te", lambda e, qc=qc, kc=kc, o=o: e.matmul(
                    psS[64:128, o:o + 64], qkv[64:128, qc + o:qc + o + 64],
                    qkv[64:128, kc + o:kc + o + 64],
                    start=True, stop=True, tile_position=(64, 64)),
                    reads=[b_qkvs[c], b_qkvs[NCH + c]], writes=[b_psS])

        def softmax(c):
            ab, b_ab = abf[c % 2]
            se, b_se = seb[c % 2]
            mx, b_mx = mxb[c % 2]
            sm, b_sm = smb[c % 2]
            rs, b_rs = rsb[c % 2]
            ps3 = psS.ap().rearrange("p (b n) -> p b n", n=64)
            p.op("ve", lambda e, ps3=ps3, mx=mx: e.tensor_reduce(mx[:, :], ps3, AX.X, ALU.max,
                                                                 negate=True),
                 reads=[b_psS], writes=[b_mx])
            for blk in range(8):
                p.op("sc", lambda e, blk=blk, se=se, mx=mx: e.activation(
                    se[:, blk * 64:(blk + 1) * 64], psS[:, blk * 64:(blk + 1) * 64],
                    AF.Exp, bias=mx[:, blk:blk + 1]),
                    reads=[b_psS, b_mx], writes=[b_se])
            p.op("ve", lambda e, se=se, sm=sm: e.tensor_reduce(
                sm[:, :], se.ap().rearrange("p (b n) -> p b n", n=64), AX.X, ALU.add),
                reads=[b_se], writes=[b_sm])
            p.op("ve", lambda e, sm=sm, rs=rs: e.reciprocal(rs[:, :], sm[:, :]),
                 reads=[b_sm], writes=[b_rs])
            p.op("ve", lambda e, ab=ab, se=se, rs=rs: e.tensor_tensor(
                ab.ap().rearrange("p (b n) -> p b n", n=64),
                se.ap().rearrange("p (b n) -> p b n", n=64), bci(rs, 0, 8), ALU.mult),
                reads=[b_se, b_rs], writes=[b_ab])

        def transpav(c):
            ab, b_ab = abf[c % 2]
            at, b_at = atT[c % 2]
            pt, b_pt = psT[c % 2]
            for blk in range(8):
                p.op("te", lambda e, blk=blk, pt=pt, ab=ab: e.transpose(
                    pt[:, blk * 128:(blk + 1) * 128], ab[:, blk * 64:blk * 64 + 64],
                    idt[:, :]),
                    reads=[b_ab, b_idt], writes=[b_pt])
            p.op("sc", lambda e, pt=pt, at=at: e.copy(at[:, :], pt[:, :]),
                 reads=[b_pt], writes=[b_at])
            for blk in range(8):
                o = blk * 64
                for h in range(2):
                    vsl = vt[0:64, blk * 1024 + c * 128 + h * 64:
                             blk * 1024 + c * 128 + h * 64 + 64]
                    asl = at[0:64, blk * 128 + h * 64: blk * 128 + h * 64 + 64]
                    p.op("te", lambda e, vsl=vsl, asl=asl, o=o, h=h: e.matmul(
                        psA[h * 64:(h + 1) * 64, o:o + 64], vsl, asl,
                        start=True, stop=True, tile_position=(0, h * 64)),
                        reads=[b_vt, b_at], writes=[b_psA])
            p.op("sc", lambda e, c=c: e.copy(avo[:, c * TN:(c + 1) * TN], psA[:, :]),
                 reads=[b_psA], writes=[b_avos[c]])

        for it in range(NCH + 3):
            if it < NCH:
                qk(it)
            if it == NCH:
                if j + 1 < NT:
                    frontend(j + 1)
            if 0 <= it - 2 < NCH:
                softmax(it - 2)
            if 0 <= it - 1 < NCH:
                scores(it - 1)
            if 0 <= it - 3 < NCH:
                transpav(it - 3)

        def evo(oc, psb, b_ps):
            p.op("ve", lambda e, oc=oc, psb=psb: e.scalar_tensor_tensor(
                R[:, oc * TN:(oc + 1) * TN], psb[:, 0:TN], bo[:, oc:oc + 1],
                R[:, oc * TN:(oc + 1) * TN], ALU.add, ALU.add),
                reads=[b_ps, b_bo], writes=[b_R])

        emit_proj(ctx, wout, b_wout, D, avo, b_avos, TN, 0, TN, pool, evo)
        p.op("sy", lambda e, col=col, R=R: e.dma_start(
            out=r_dram(rdst, col, TN), in_=ap3(R, TN, NCH, 0, TN)),
            reads=[b_R], dma=True)

    frontend(0)
    for j in range(NT):
        backend(j)


def emit_ffn1_phase(ctx, dr, rsrc, rdst, comm, flagidx, hsh):
    p = ctx.p
    wr, b_wr = fload_weight(ctx, "p_wr1", dr["wrT1"], D, D, BF16)
    wk, b_wk = fload_weight(ctx, "p_wk1", dr["wkT1"], D, D, BF16)
    wv, b_wv = fload_weight(ctx, "p_wv1", dr["wvT1"], D, D, BF16)
    mur, b_mur = fload_vec(ctx, "p_mur", dr["mur_l"], NCH)
    muk, b_muk = fload_vec(ctx, "p_muk", dr["muk_l"], NCH)
    zm, b_zm = fload_vec(ctx, "p_zmask", dr["zmask"], NCH)
    ln = LN(ctx)
    Rt = [ctx.sbuf(f"p_Rt{i}", [128, NCH * TN], F32) for i in range(2)]
    nmu = [ctx.sbuf(f"p_negmu{i}", [128, TN], F32) for i in range(2)]
    avv = [ctx.sbuf(f"p_aval{i}", [128, TN], F32) for i in range(2)]
    bxn, b_bxn = ctx.sbuf("p_bxn", [128, NCH], BF16)
    bmu, b_bmu = ctx.sbuf("p_bmu", [128, 1], F32)
    bav, b_bav = ctx.sbuf("p_bav", [128, 1], F32)
    bt, b_bt = ctx.sbuf("p_bt", [128, NCH], F32)
    hrecv, b_hrecv = ctx.sbuf("p_hrecv", [128, NCH], BF16)
    carry, b_carry = ctx.sbuf("p_carry", [128, NCH], BF16)
    XW = TN + 1
    xeb = [ctx.sbuf(f"p_xe{i}", [128, NCH * XW], BF16) for i in range(2)]
    dt, b_dt = ctx.sbuf("p_dt", [128, NCH * TN], BF16)
    gx, b_gx = ctx.sbuf("p_gx", [128, TN], BF16)
    xr, b_xr = ctx.sbuf("p_xr", [128, NCH * TN], BF16)
    xk, b_xk = ctx.sbuf("p_xk", [128, NCH * TN], BF16)
    rg, b_rg = ctx.sbuf("p_rg", [128, NCH * TN], BF16)
    kact, b_kact = ctx.sbuf("p_kact", [128, NCH * TN], BF16)
    tv, b_tv = ctx.sbuf("p_tv", [128, TN], F32)
    krel, b_krel = tv, b_tv
    pools = [ctx.psum(f"p_pp{i}", [128, TN], F32) for i in range(4)]

    p.op("sy", lambda e: e.dma_start(out=ap3(bt, 1, NCH, 0, 1),
                                     in_=r_dram(rsrc, T - 1, 1),
                                     allow_slow_non_contiguous=True),
         writes=[b_bt], dma=True)
    ln.stats(bt, b_bt, 1, 0, 1, bmu, b_bmu, 0, bav, b_bav, 0)
    ln.apply(bt, b_bt, 1, 0, 1, bmu, b_bmu, 0, bav, b_bav, 0, bxn, b_bxn, 1, 0)

    def wr_h(e, slot):
        return e.dma_start(out=hsh[slot * 128:(slot + 1) * 128, :], in_=bxn[:, :])
    comm.send(wr_h, [b_bxn], flagidx)
    ptok = comm.wait_partner(flagidx)
    p.op("gp", lambda e: e.dma_start(out=hrecv[:, :], in_=hsh[0:128, :]),
         writes=[b_hrecv], deps=[ptok], dma=True)
    p.op("ve", lambda e: e.tensor_tensor(carry[:, :], hrecv[:, :], zm[:, :], ALU.mult),
         reads=[b_hrecv, b_zm], writes=[b_carry])

    def frontend(j):
        par = j % 2
        col = j * TN
        R, b_R = Rt[par]
        negmu, b_negmu = nmu[par]
        avl, b_avl = avv[par]
        xe, b_xe = xeb[par]
        p.op("sy", lambda e, col=col, R=R: e.dma_start(
            out=ap3(R, TN, NCH, 0, TN), in_=r_dram(rsrc, col, TN)),
            writes=[b_R], dma=True)
        ln.stats(R, b_R, TN, 0, TN, negmu, b_negmu, 0, avl, b_avl, 0, par=par)
        p.op("ve", lambda e, xe=xe: e.tensor_copy(ap3(xe, XW, NCH, 0, 1),
                                                  ap3(carry, 1, NCH, 0, 1)),
             reads=[b_carry], writes=[b_xe])
        ln.apply(R, b_R, TN, 0, TN, negmu, b_negmu, 0, avl, b_avl, 0,
                 xe, b_xe, XW, 1, par=par)
        p.op("ve", lambda e, xe=xe: e.tensor_copy(ap3(carry, 1, NCH, 0, 1),
                                                  ap3(xe, XW, NCH, TN, 1)),
             reads=[b_xe], writes=[b_carry])

    def backend(j):
        par = j % 2
        col = j * TN
        R, b_R = Rt[par]
        xe, b_xe = xeb[par]
        cur = ap3(xe, XW, NCH, 1, TN)
        prv = ap3(xe, XW, NCH, 0, TN)
        p.op("ve", lambda e, cur=cur, prv=prv: e.tensor_tensor(
            ap3(dt, TN, NCH, 0, TN), cur, prv, ALU.subtract),
            reads=[b_xe], writes=[b_dt])
        for (mu, b_mu, xdst, b_xdst) in ((mur, b_mur, xr, b_xr), (muk, b_muk, xk, b_xk)):
            for ic in range(NCH):
                p.op("ve", lambda e, ic=ic, mu=mu: e.tensor_scalar(
                    gx[:, :], dt[:, ic * TN:(ic + 1) * TN],
                    mu[:, ic:ic + 1], None, ALU.mult),
                    reads=[b_dt, b_mu], writes=[b_gx])
                p.op("ve", lambda e, ic=ic, xdst=xdst, xe=xe: e.tensor_tensor(
                    xdst[:, ic * TN:(ic + 1) * TN], gx[:, :],
                    bass.AP(xe, ic * XW, [[xe.ap().ap[0][0], 128], [1, TN]]), ALU.add),
                    reads=[b_gx, b_xe], writes=[b_xdst])

        def ev_r(oc, psb, b_ps):
            p.op("sc", lambda e, oc=oc, psb=psb: e.activation(
                rg[:, oc * TN:(oc + 1) * TN], psb[:, 0:TN], AF.Sigmoid),
                reads=[b_ps], writes=[b_rg])

        emit_proj(ctx, wr, b_wr, D, xr, b_xr, TN, 0, TN, pools[0:2], ev_r)

        def ev_k(oc, psb, b_ps):
            p.op("ve", lambda e, psb=psb: e.tensor_scalar(
                krel[:, :], psb[:, 0:TN], 0.0, None, ALU.max),
                reads=[b_ps], writes=[b_krel])
            p.op("ve", lambda e, oc=oc: e.tensor_tensor(
                kact[:, oc * TN:(oc + 1) * TN], krel[:, :], krel[:, :], ALU.mult),
                reads=[b_krel], writes=[b_kact])

        emit_proj(ctx, wk, b_wk, D, xk, b_xk, TN, 0, TN, pools[2:4], ev_k)

        def ev_v(oc, psb, b_ps, R=R, b_R=b_R):
            p.op("ve", lambda e, oc=oc, psb=psb: e.tensor_tensor(
                tv[:, :], psb[:, 0:TN], rg[:, oc * TN:(oc + 1) * TN], ALU.mult),
                reads=[b_ps, b_rg], writes=[b_tv])
            p.op("ve", lambda e, oc=oc: e.scalar_tensor_tensor(
                R[:, oc * TN:(oc + 1) * TN], tv[:, :], 0.5,
                R[:, oc * TN:(oc + 1) * TN], ALU.mult, ALU.add),
                reads=[b_tv], writes=[b_R])

        emit_proj(ctx, wv, b_wv, D, kact, b_kact, TN, 0, TN, pools[0:2], ev_v)
        p.op("sy", lambda e, col=col, R=R: e.dma_start(
            out=r_dram(rdst, col, TN), in_=ap3(R, TN, NCH, 0, TN)),
            reads=[b_R], dma=True)

    frontend(0)
    for j in range(NT):
        if j + 1 < NT:
            frontend(j + 1)
        backend(j)


def emit_lin_phase(ctx, dr, rsrc, rdst, comm):
    p = ctx.p
    wk, b_wk = fload_weight(ctx, "p_wkla", dr["wkT"], D, 128, BF16)
    wq, b_wq = fload_weight(ctx, "p_wqla", dr["wqT"], D, 128, BF16)
    idt, b_idt = fload_weight(ctx, "p_ident2", dr["ident"], 128, 128, BF16)
    gam, b_gam = fload_vec(ctx, "p_gam", dr["gam_l"], NCH)
    bet, b_bet = fload_vec(ctx, "p_bet", dr["bet_l"], NCH)
    ln = LN(ctx, nbuf=2)
    Rt = [ctx.sbuf(f"p_Rt{i}", [128, NCH * TN], F32) for i in range(2)]
    xnb = [ctx.sbuf(f"p_xn{i}", [128, NCH * TN], BF16) for i in range(2)]
    nmu = [ctx.sbuf(f"p_negmu{i}", [128, TN], F32) for i in range(2)]
    avv = [ctx.sbuf(f"p_aval{i}", [128, TN], F32) for i in range(2)]
    ktok, b_ktok = ctx.sbuf("p_ktok", [128, 128], BF16)
    jtok, b_jtok = ctx.sbuf("p_jtok", [128, 1024], BF16)
    msb, b_msb = ctx.sbuf("p_msb", [128, 1024], F32)
    m0r, b_m0r = ctx.sbuf("p_m0r", [128, 1024], F32)
    m1sb, b_m1sb = ctx.sbuf("p_m1sb", [128, 1024], F32)
    mbf, b_mbf = ctx.sbuf("p_mbf", [128, 1024], BF16)
    qbf, b_qbf = ctx.sbuf("p_qbf", [128, TN], BF16)
    jn, b_jn = ctx.sbuf("p_jn", [128, NCH * TN], F32)
    nmu2 = [ctx.sbuf(f"p_nmu2{i}", [128, TN], F32) for i in range(2)]
    av2 = [ctx.sbuf(f"p_av2{i}", [128, TN], F32) for i in range(2)]
    g1, b_g1 = ctx.sbuf("p_g1", [128, TN], F32)
    psK, b_psK = ctx.psum("p_psK", [128, 128], F32)
    psJ, b_psJ = ctx.psum("p_psJ", [128, 1024], BF16)
    psM = [ctx.psum(f"p_psM{i}", [128, TN], F32) for i in range(2)]
    pools = psM

    nblk = T // 128

    def m_frontend(j):
        par = j % 2
        col = j * TN
        R, b_R = Rt[par]
        xn, b_xn = xnb[par]
        negmu, b_negmu = nmu[par]
        avl, b_avl = avv[par]
        p.op("sy", lambda e, col=col, R=R: e.dma_start(
            out=ap3(R, TN, NCH, 0, TN), in_=r_dram(rsrc, col, TN)),
            writes=[b_R], dma=True)
        ln.stats(R, b_R, TN, 0, TN, negmu, b_negmu, 0, avl, b_avl, 0, par=par)
        ln.apply(R, b_R, TN, 0, TN, negmu, b_negmu, 0, avl, b_avl, 0, xn, b_xn, TN, 0, par=par)
        p.op("sy", lambda e, col=col, xn=xn: e.dma_start(
            out=bass.AP(dr["xstash"], col, [[T, 128], [128 * T, NCH], [1, TN]]),
            in_=ap3(xn, TN, NCH, 0, TN)),
            reads=[b_xn], dma=True)

    def m_backend(j):
        par = j % 2
        xn, b_xn = xnb[par]
        for tb in range(TN // 128):
            gi = j * (TN // 128) + tb
            for ic in range(NCH):
                p.op("te", lambda e, ic=ic, tb=tb, xn=xn: e.matmul(
                    psK[:, :], xn[:, ic * TN + tb * 128: ic * TN + (tb + 1) * 128],
                    wk[:, ic * 128:(ic + 1) * 128],
                    start=(ic == 0), stop=(ic == NCH - 1)),
                    reads=[b_xn, b_wk], writes=[b_psK],
                    inc=(1 if ic == NCH - 1 else 0))
            p.op("sc", lambda e: e.copy(ktok[:, :], psK[:, :]), reads=[b_psK], writes=[b_ktok])
            for ic in range(NCH):
                p.op("te", lambda e, ic=ic, tb=tb, xn=xn: e.transpose(
                    psJ[:, ic * 128:(ic + 1) * 128],
                    xn[:, ic * TN + tb * 128: ic * TN + (tb + 1) * 128], idt[:, :]),
                    reads=[b_xn, b_idt], writes=[b_psJ])
            p.op("sc", lambda e: e.copy(jtok[:, :], psJ[:, :]), reads=[b_psJ], writes=[b_jtok])
            for half in range(2):
                psb, b_psb = psM[half]
                p.op("te", lambda e, psb=psb, half=half, s=(gi == 0), t=(gi == nblk - 1): e.matmul(
                    psb[:, :], ktok[:, :], jtok[:, half * TN:(half + 1) * TN],
                    start=s, stop=t),
                    reads=[b_ktok, b_jtok], writes=[b_psb])

    m_frontend(0)
    for j in range(NT):
        if j + 1 < NT:
            m_frontend(j + 1)
        m_backend(j)

    for half in range(2):
        psb, b_psb = psM[half]
        p.op("sc", lambda e, half=half, psb=psb: e.copy(
            msb[:, half * TN:(half + 1) * TN], psb[:, :]),
            reads=[b_psb], writes=[b_msb])

    def wr_m(e, slot):
        return e.dma_start(out=dr["msh"][slot * 128:(slot + 1) * 128, :], in_=msb[:, :])
    comm.send(wr_m, [b_msb], 1)
    ptok = comm.wait_partner(1)
    p.op("gp", lambda e: e.dma_start(out=m0r[:, :], in_=dr["msh"][0:128, :]),
         writes=[b_m0r], deps=[ptok], dma=True)
    p.op("gp", lambda e: e.dma_start(out=m1sb[:, :], in_=dr["msh"][128:256, :]),
         writes=[b_m1sb], deps=[ptok], dma=True)
    p.op("ve", lambda e: e.tensor_tensor(mbf[:, :], m0r[:, :], m1sb[:, :], ALU.add),
         reads=[b_m0r, b_m1sb], writes=[b_mbf])

    def a_frontend(j):
        par = j % 2
        col = j * TN
        R, b_R = Rt[par]
        xn, b_xn = xnb[par]
        p.op("sy", lambda e, col=col, R=R: e.dma_start(
            out=ap3(R, TN, NCH, 0, TN), in_=r_dram(rsrc, col, TN)),
            writes=[b_R], dma=True)
        p.op("sy", lambda e, col=col, xn=xn: e.dma_start(
            out=ap3(xn, TN, NCH, 0, TN),
            in_=bass.AP(dr["xstash"], col, [[T, 128], [128 * T, NCH], [1, TN]])),
            writes=[b_xn], dma=True)

    def a_backend(j):
        par = j % 2
        col = j * TN
        R, b_R = Rt[par]
        xn, b_xn = xnb[par]
        negmu2, b_negmu2 = nmu2[par]
        avl2, b_avl2 = av2[par]

        def evq(oc, psb, b_ps):
            p.op("sc", lambda e, psb=psb: e.copy(qbf[:, :], psb[:, 0:TN]),
                 reads=[b_ps], writes=[b_qbf])

        emit_proj(ctx, wq, b_wq, 128, xn, b_xn, TN, 0, TN, pools, evq)
        for dc in range(NCH):
            psb, b_psb = pools[dc % 2]
            p.op("te", lambda e, dc=dc, psb=psb: e.matmul(
                psb[:, 0:TN], mbf[:, dc * 128:(dc + 1) * 128], qbf[:, :],
                start=True, stop=True),
                reads=[b_mbf, b_qbf], writes=[b_psb])
            p.op("ve", lambda e, dc=dc, psb=psb, xn=xn: e.scalar_tensor_tensor(
                jn[:, dc * TN:(dc + 1) * TN], psb[:, 0:TN], RSQ_DK,
                xn[:, dc * TN:(dc + 1) * TN], ALU.mult, ALU.add),
                reads=[b_psb, b_xn], writes=[b_jn])
        ln.stats(jn, b_jn, TN, 0, TN, negmu2, b_negmu2, 0, avl2, b_avl2, 0, par=par)
        scr, b_scr = ln.scr[par]
        p.op("ve", lambda e, scr=scr: e.tensor_tensor(
            ap3(scr, TN, NCH, 0, TN), ap3(jn, TN, NCH, 0, TN),
            bc3(negmu2, 0, TN, NCH), ALU.add),
            reads=[b_jn, b_negmu2], writes=[b_scr])
        p.op("ve", lambda e, scr=scr: e.tensor_tensor(
            ap3(scr, TN, NCH, 0, TN), ap3(scr, TN, NCH, 0, TN),
            bc3(avl2, 0, TN, NCH), ALU.mult),
            reads=[b_avl2], writes=[b_scr])
        for ic in range(NCH):
            p.op("ve", lambda e, ic=ic, scr=scr: e.tensor_scalar(
                g1[:, :], scr[:, ic * TN:(ic + 1) * TN],
                gam[:, ic:ic + 1], bet[:, ic:ic + 1], ALU.mult, ALU.add),
                reads=[b_scr, b_gam, b_bet], writes=[b_g1])
            p.op("ve", lambda e, ic=ic, R=R: e.tensor_tensor(
                R[:, ic * TN:(ic + 1) * TN], R[:, ic * TN:(ic + 1) * TN],
                g1[:, :], ALU.add),
                reads=[b_g1], writes=[b_R])
        p.op("sy", lambda e, col=col, R=R: e.dma_start(
            out=r_dram(rdst, col, TN), in_=ap3(R, TN, NCH, 0, TN)),
            reads=[b_R], dma=True)

    a_frontend(0)
    for j in range(NT):
        if j + 1 < NT:
            a_frontend(j + 1)
        a_backend(j)


def build_fused():
    nc = bass.Bass("TRN2", target_bir_lowering=False)
    with contextlib.ExitStack() as st0:
        ctx = Ctx(nc, st0)
        p = ctx.p
        dr = {}
        rin = nc.dram_tensor("r", [D, T], F32, kind="ExternalInput")
        rout = nc.dram_tensor("rout", [D, T], F32, kind="ExternalOutput")
        for key, shape in (("wrT", [D, D]), ("wiT", [D, D]), ("fr", [D, D]), ("nfi", [D, D]),
                           ("grh", [D, 512]), ("ngih", [D, 512]), ("gih", [D, 512]),
                           ("bias_l", [128, NCH]),
                           ("wqkvT", [D, 3 * D]), ("bqkv_l", [128, 3 * NCH]),
                           ("woutT", [D, D]), ("bout_l", [128, NCH]),
                           ("ident", [128, 128]),
                           ("wrT1", [D, D]), ("wkT1", [D, D]), ("wvT1", [D, D]),
                           ("mur_l", [128, NCH]), ("muk_l", [128, NCH]),
                           ("zmask", [128, NCH]),
                           ("wqT", [D, 128]), ("wkT", [D, 128]),
                           ("gam_l", [128, NCH]), ("bet_l", [128, NCH]),
                           ("bv_row", [1, 1024])):
            dr[key] = nc.dram_tensor(key, shape, F32, kind="ExternalInput")
        dr["m0sh"] = nc.dram_tensor("m0sh", [D, D], F32, addr_space="Shared")
        dr["msh"] = nc.dram_tensor("msh", [256, D], F32, addr_space="Shared")
        dr["hsh1"] = nc.dram_tensor("hsh1", [256, NCH], BF16, addr_space="Shared")
        dr["hsh2"] = nc.dram_tensor("hsh2", [256, NCH], BF16, addr_space="Shared")
        dr["xstash"] = nc.dram_tensor("xstash", [D, T], BF16)
        ra = nc.dram_tensor("ra", [D, T], F32)
        rb = nc.dram_tensor("rb", [D, T], F32)

        comm = PairComm(ctx, st0)
        sems = {name: st0.enter_context(nc.semaphore(name))
                for name in ("te", "sc", "ve", "gp", "dsy", "dsc", "dgp", "dve")}

        def phase(fn, *args):
            with contextlib.ExitStack() as pst:
                old = ctx.stack
                ctx.stack = pst
                fn(ctx, dr, *args)
                p.phase_barrier()
                p.flush(nc, sems)
                ctx.stack = old

        phase(emit_m0_phase, comm)
        phase(emit_ffn0_phase, rin, rb)
        phase(emit_attn0_phase, rb, ra)
        phase(emit_ffn0_phase, ra, rb)
        phase(emit_ffn1_phase, rb, ra, comm, 2, dr["hsh1"])
        phase(emit_lin_phase, ra, rb, comm)
        phase(emit_ffn1_phase, rb, rout, comm, 3, dr["hsh2"])

        p.finish()
        p.flush(nc, sems)
    return nc


def kernel_fused(L0_attn_x, L0_attn_W_qkv, L0_attn_b_qkv, L0_attn_W_out, L0_attn_b_out,
                 L0_ffn_w_freq_real, L0_ffn_w_freq_imag, L0_ffn_bias,
                 L1_attn_W_Q, L1_attn_W_K, L1_attn_gamma, L1_attn_beta,
                 L1_ffn_W_r, L1_ffn_W_k, L1_ffn_W_v, L1_ffn_mu_r, L1_ffn_mu_k):
    cores = list(range(8))
    x = np.asarray(L0_attn_x, np.float32)
    Fr, Fi, Gr, Gi = _dft_constants()
    nfi = np.ascontiguousarray(-Fi)
    wrT = np.ascontiguousarray(np.asarray(L0_ffn_w_freq_real, np.float32).T)
    wiT = np.ascontiguousarray(np.asarray(L0_ffn_w_freq_imag, np.float32).T)
    wqkvT = np.ascontiguousarray(np.asarray(L0_attn_W_qkv, np.float32).T)
    woutT = np.ascontiguousarray(np.asarray(L0_attn_W_out, np.float32).T)
    wrT1 = np.ascontiguousarray(np.asarray(L1_ffn_W_r, np.float32).T)
    wkT1 = np.ascontiguousarray(np.asarray(L1_ffn_W_k, np.float32).T)
    wvT1 = np.ascontiguousarray(np.asarray(L1_ffn_W_v, np.float32).T)
    wqT = np.ascontiguousarray(np.asarray(L1_attn_W_Q, np.float32).T)
    wkT = np.ascontiguousarray(np.asarray(L1_attn_W_K, np.float32).T)
    ident = np.eye(128, dtype=np.float32)
    common = {
        "wrT": wrT, "wiT": wiT, "fr": Fr, "nfi": nfi,
        "bias_l": _vec_l(np.asarray(L0_ffn_bias, np.float32)),
        "wqkvT": wqkvT, "bqkv_l": _vec_l(np.asarray(L0_attn_b_qkv, np.float32)),
        "woutT": woutT, "bout_l": _vec_l(np.asarray(L0_attn_b_out, np.float32)),
        "bv_row": np.ascontiguousarray(np.asarray(L0_attn_b_qkv, np.float32)[2048:3072].reshape(1, 1024)),
        "ident": ident,
        "wrT1": wrT1, "wkT1": wkT1, "wvT1": wvT1,
        "mur_l": _vec_l(np.asarray(L1_ffn_mu_r, np.float32)),
        "muk_l": _vec_l(np.asarray(L1_ffn_mu_k, np.float32)),
        "wqT": wqT, "wkT": wkT,
        "gam_l": _vec_l(np.asarray(L1_attn_gamma, np.float32)),
        "bet_l": _vec_l(np.asarray(L1_attn_beta, np.float32)),
    }
    nc = build_fused()
    in_maps = []
    for c in cores:
        b, h = c // 2, c % 2
        sl = slice(h * 512, (h + 1) * 512)
        m = dict(common)
        m["r"] = np.ascontiguousarray(x[b, h * T:(h + 1) * T, :].T)
        m["grh"] = np.ascontiguousarray(Gr[:, sl])
        m["ngih"] = np.ascontiguousarray(-Gi[:, sl])
        m["gih"] = np.ascontiguousarray(Gi[:, sl])
        m["cid"] = np.array([[h]], np.int32)
        m["zmask"] = np.zeros((128, NCH), np.float32) if h == 0 else np.ones((128, NCH), np.float32)
        in_maps.append(m)
    res = run_bass_kernel_spmd(nc, in_maps, core_ids=cores)
    out = np.empty_like(x)
    for c in cores:
        b, h = c // 2, c % 2
        out[b, h * T:(h + 1) * T, :] = res.results[c]["rout"].T
    return out


# version 18
# speedup vs baseline: 2.3991x; 2.3991x over previous
import contextlib
import numpy as np
import ml_dtypes
import concourse.bass as bass
import concourse.mybir as mybir
from concourse.bass_utils import run_bass_kernel_spmd

F32 = mybir.dt.float32
F32R = mybir.dt.float32r
BF16 = mybir.dt.bfloat16
AF = mybir.ActivationFunctionType
ALU = mybir.AluOpType
AX = mybir.AxisListType
BFNP = ml_dtypes.bfloat16

D = 1024
NCH = 8
T = 4096
NT = 8
TN = 512
EPS = 1e-5
RSQ_DK = float(1.0 / np.sqrt(128.0))

ENGS = ("te", "sc", "ve", "gp", "sy")
DMA_SEM = {"sy": "dsy", "sc": "dsc", "gp": "dgp", "ve": "dve"}


class Buf:
    __slots__ = ("tok_w", "toks_r", "name")

    def __init__(self, name=""):
        self.tok_w = None
        self.toks_r = []
        self.name = name


class Prog:
    def __init__(self):
        self.stream = {e: [] for e in ENGS}
        self.cnt = {}
        self.waited = {e: {} for e in ENGS}

    def _emit_wait(self, eng, tok):
        if tok is None:
            return
        sem, val = tok
        w = self.waited[eng]
        if w.get(sem, 0) >= val:
            return
        w[sem] = val
        self.stream[eng].append(("wait", sem, val))

    def op(self, eng, build, reads=(), writes=(), deps=(), dma=False, inc=None):
        toks = []
        for b in reads:
            toks.append(b.tok_w)
        for b in writes:
            toks.append(b.tok_w)
            toks.extend(b.toks_r)
        toks.extend(deps)
        for t in toks:
            self._emit_wait(eng, t)
        if dma:
            sem = DMA_SEM[eng]
            step = 16 if inc is None else inc
        else:
            sem = eng
            step = 1 if inc is None else inc
        if step:
            after = self.cnt.get(sem, 0) + step
            self.cnt[sem] = after
            tok = (sem, after)
        else:
            tok = None
        self.stream[eng].append(("inst", build, sem if step else None, step))
        for b in reads:
            if tok is not None:
                b.toks_r.append(tok)
        for b in writes:
            if tok is not None:
                b.tok_w = tok
                b.toks_r = []
        return tok

    def finish(self):
        for t in [(s, v) for s, v in self.cnt.items()]:
            self._emit_wait("sy", t)

    def phase_barrier(self):
        toks = [(s, v) for s, v in self.cnt.items()]
        for e in ENGS:
            for t in toks:
                self._emit_wait(e, t)

    def flush(self, nc, sems):
        if not any(self.stream[e] for e in ENGS):
            return
        with nc.Block() as block:
            for ename, attr in (("te", "tensor"), ("sc", "scalar"), ("ve", "vector"),
                                ("gp", "gpsimd"), ("sy", "sync")):
                items = self.stream[ename]
                if not items:
                    continue

                def make(items):
                    def body(eng):
                        for it in items:
                            if it[0] == "wait":
                                eng.wait_ge(sems[it[1]], it[2])
                            else:
                                _, build, sem, step = it
                                if getattr(build, "needs_sems", False):
                                    build(eng, sems)
                                else:
                                    inst = build(eng)
                                    if sem is not None:
                                        inst.then_inc(sems[sem], step)
                    return body

                getattr(block, attr)(make(items))
        self.stream = {e: [] for e in ENGS}

    def realize(self, nc):
        semnames = sorted(self.cnt.keys())
        with contextlib.ExitStack() as st:
            sems = {s: st.enter_context(nc.semaphore(s)) for s in semnames}
            self.flush(nc, sems)


class Ctx:
    def __init__(self, nc, stack):
        self.nc = nc
        self.p = Prog()
        self.stack = stack

    _uid = 0

    def _u(self, name):
        Ctx._uid += 1
        return f"{name}_{Ctx._uid}"

    def sbuf(self, name, shape, dtype):
        t = self.stack.enter_context(self.nc.sbuf_tensor(self._u(name), shape, dtype))
        return t, Buf(name)

    def psum(self, name, shape, dtype):
        t = self.stack.enter_context(self.nc.psum_tensor(self._u(name), shape, dtype))
        return t, Buf(name)


def ap3(t, stride, nch, col0, ncol):
    row = t.ap().ap[0][0]
    return bass.AP(t, col0, [[row, 128], [stride, nch], [1, ncol]])


def bc3(t, col0, ncol, nch):
    row = t.ap().ap[0][0]
    return bass.AP(t, col0, [[row, 128], [0, nch], [1, ncol]])


def bci(t, col0, nb):
    row = t.ap().ap[0][0]
    return bass.AP(t, col0, [[row, 128], [1, nb], [0, 64]])


def load_weight(ctx, name, key, din, dout, dtype=BF16):
    nc, p = ctx.nc, ctx.p
    dr = nc.dram_tensor(key, [din, dout], F32, kind="ExternalInput")
    sb, buf = ctx.sbuf("sb_" + name, [128, (din // 128) * dout], dtype)
    src = bass.AP(dr, 0, [[dout, 128], [128 * dout, din // 128], [1, dout]])
    dst = ap3(sb, dout, din // 128, 0, dout)
    eng = "gp" if dtype == BF16 else "sy"
    p.op(eng, lambda e, d=dst, s=src: e.dma_start(out=d, in_=s), writes=[buf], dma=True)
    return sb, buf


def wblk(sb, ic, oc, dout):
    return sb[:, ic * dout + oc * 128: ic * dout + (oc + 1) * 128]


def load_vec(ctx, name, key, n):
    nc, p = ctx.nc, ctx.p
    dr = nc.dram_tensor(key, [128, n], F32, kind="ExternalInput")
    sb, buf = ctx.sbuf("sb_" + name, [128, n], F32)
    p.op("sy", lambda e, d=sb, s=dr: e.dma_start(out=d[:, :], in_=s[:, :]), writes=[buf], dma=True)
    return sb, buf


def r_dram(dr, col0, ncol):
    return bass.AP(dr, col0, [[T, 128], [128 * T, NCH], [1, ncol]])


class LN:
    _n = 0

    def __init__(self, ctx, nbuf=2, scr=True, cpe="ve"):
        c = ctx
        self.ctx = ctx
        self.nbuf = nbuf
        self.cpe = cpe
        LN._n += 1
        pf = f"ln{LN._n}_"
        self.rbf = [c.sbuf(pf + f"rbf{i}", [128, NCH * TN], BF16) for i in range(nbuf)]
        self.rsq = [c.sbuf(pf + f"rsq{i}", [128, NCH * TN], BF16) for i in range(nbuf)]
        _scr = c.sbuf(pf + "scr", [128, NCH * TN], F32) if scr else None
        self.scr = [_scr for i in range(nbuf)]
        self.m2 = [c.sbuf(pf + f"m2{i}", [128, TN], F32) for i in range(nbuf)]
        self.ones, self.b_ones = c.sbuf(pf + "ones", [128, 128], BF16)
        self.psS = [c.psum(pf + f"psS{i}", [128, TN], F32) for i in range(nbuf)]
        self.psQ = [c.psum(pf + f"psQ{i}", [128, TN], F32) for i in range(nbuf)]
        ctx.p.op("gp", lambda e: e.memset(self.ones[:, :], 1.0), writes=[self.b_ones])

    def stats(self, R, b_R, rstride, rcol, n, negmu, b_negmu, mucol, a, b_a, acol, par=0):
        p = self.ctx.p
        par = par % self.nbuf
        rbf, b_rbf = self.rbf[par]
        rsq, b_rsq = self.rsq[par]
        m2, b_m2 = self.m2[par]
        psS, b_psS = self.psS[par]
        psQ, b_psQ = self.psQ[par]
        src = ap3(R, rstride, NCH, rcol, n)
        p.op("sc", lambda e: e.activation(ap3(rsq, TN, NCH, 0, n), src, AF.Square),
             reads=[b_R], writes=[b_rsq])
        p.op(self.cpe, lambda e: e.tensor_copy(ap3(rbf, TN, NCH, 0, n), src),
             reads=[b_R], writes=[b_rbf])
        for ic in range(NCH):
            p.op("te", lambda e, ic=ic: e.matmul(
                psS[:, 0:n], self.ones[:, :], rbf[:, ic * TN: ic * TN + n],
                start=(ic == 0), stop=(ic == NCH - 1)),
                reads=[self.b_ones, b_rbf], writes=[b_psS],
                inc=(1 if ic == NCH - 1 else 0))
        for ic in range(NCH):
            p.op("te", lambda e, ic=ic: e.matmul(
                psQ[:, 0:n], self.ones[:, :], rsq[:, ic * TN: ic * TN + n],
                start=(ic == 0), stop=(ic == NCH - 1)),
                reads=[self.b_ones, b_rsq], writes=[b_psQ],
                inc=(1 if ic == NCH - 1 else 0))
        p.op("sc", lambda e: e.activation(negmu[:, mucol:mucol + n], psS[:, 0:n],
                                          AF.Copy, scale=-1.0 / D),
             reads=[b_psS], writes=[b_negmu])
        p.op("sc", lambda e: e.activation(m2[:, 0:n], negmu[:, mucol:mucol + n], AF.Square),
             reads=[b_negmu], writes=[b_m2])
        p.op("ve", lambda e: e.scalar_tensor_tensor(
            a[:, acol:acol + n], psQ[:, 0:n], 1.0 / D, m2[:, 0:n],
            ALU.mult, ALU.subtract),
            reads=[b_psQ, b_m2], writes=[b_a])
        p.op("ve", lambda e: e.tensor_scalar(
            a[:, acol:acol + n], a[:, acol:acol + n], float(EPS), None, ALU.add),
            reads=[b_a], writes=[b_a])
        p.op("sc", lambda e: e.activation(a[:, acol:acol + n], a[:, acol:acol + n], AF.Ln),
             reads=[b_a], writes=[b_a])
        p.op("sc", lambda e: e.activation(a[:, acol:acol + n], a[:, acol:acol + n], AF.Exp,
                                          scale=-0.5),
             reads=[b_a], writes=[b_a])

    def apply(self, R, b_R, rstride, rcol, n, negmu, b_negmu, mucol, a, b_a, acol,
              xn, b_xn, xstride, xcol, par=0):
        p = self.ctx.p
        par = par % self.nbuf
        scr, b_scr = self.scr[par]
        src = ap3(R, rstride, NCH, rcol, n)
        p.op("ve", lambda e: e.tensor_tensor(
            ap3(scr, TN, NCH, 0, n), src, bc3(negmu, mucol, n, NCH), ALU.add),
            reads=[b_R, b_negmu], writes=[b_scr])
        p.op("ve", lambda e: e.tensor_tensor(
            ap3(xn, xstride, NCH, xcol, n), ap3(scr, TN, NCH, 0, n),
            bc3(a, acol, n, NCH), ALU.mult),
            reads=[b_scr, b_a], writes=[b_xn])

    def apply_bf(self, par, n, negmu, b_negmu, mucol, a, b_a, acol,
                 xn, b_xn, xstride, xcol):
        p = self.ctx.p
        par = par % self.nbuf
        rbf, b_rbf = self.rbf[par]
        xbufs = b_xn if isinstance(b_xn, list) else [b_xn]
        dst = ap3(xn, xstride, NCH, xcol, n)
        p.op("ve", lambda e, rbf=rbf, dst=dst: e.tensor_tensor(
            dst, ap3(rbf, TN, NCH, 0, n), bc3(negmu, mucol, n, NCH), ALU.add),
            reads=[b_rbf, b_negmu], writes=xbufs)
        p.op("ve", lambda e, dst=dst: e.tensor_tensor(
            dst, dst, bc3(a, acol, n, NCH), ALU.mult),
            reads=[b_a] + xbufs, writes=xbufs)

    def apply_chunked(self, R, b_R, rstride, rcol, n, negmu, b_negmu, mucol,
                      a, b_a, acol, xn, b_xn, xstride, xcol, scr, b_scr):
        p = self.ctx.p
        rrow = R.ap().ap[0][0]
        xrow = xn.ap().ap[0][0]
        xbufs = b_xn if isinstance(b_xn, list) else [b_xn] * NCH
        for ic in range(NCH):
            rsl = bass.AP(R, ic * rstride + rcol, [[rrow, 128], [1, n]])
            xsl = bass.AP(xn, ic * xstride + xcol, [[xrow, 128], [1, n]])
            p.op("ve", lambda e, rsl=rsl: e.tensor_tensor(
                scr[:, 0:n], rsl, negmu[:, mucol:mucol + n], ALU.add),
                reads=[b_R, b_negmu], writes=[b_scr])
            p.op("ve", lambda e, xsl=xsl: e.tensor_tensor(
                xsl, scr[:, 0:n], a[:, acol:acol + n], ALU.mult),
                reads=[b_scr, b_a], writes=[xbufs[ic]])


def emit_proj(ctx, wsb, b_w, dout, xn, b_xn, xstride, xcol, n, pools, evac):
    p = ctx.p
    noc = dout // 128
    xrow = xn.ap().ap[0][0]
    xbufs = b_xn if isinstance(b_xn, list) else [b_xn] * NCH
    for oc in range(noc):
        ps, b_ps = pools[oc % len(pools)]
        for ic in range(NCH):
            rhs = bass.AP(xn, ic * xstride + xcol, [[xrow, 128], [1, n]])
            p.op("te", lambda e, ic=ic, oc=oc, ps=ps, rhs=rhs: e.matmul(
                ps[:, 0:n], wblk(wsb, ic, oc, dout), rhs,
                start=(ic == 0), stop=(ic == NCH - 1)),
                reads=[b_w, xbufs[ic]], writes=[b_ps],
                inc=(1 if ic == NCH - 1 else 0))
        evac(oc, ps, b_ps)



def build_m0():
    nc = bass.Bass("TRN2", target_bir_lowering=False)
    with contextlib.ExitStack() as st:
        ctx = Ctx(nc, st)
        p = ctx.p
        wrT, b_wrT = load_weight(ctx, "wrT", "wrT", D, D, F32)
        wiT, b_wiT = load_weight(ctx, "wiT", "wiT", D, D, F32)
        fr, b_fr = load_weight(ctx, "fr", "fr", D, D, F32)
        nfi, b_nfi = load_weight(ctx, "nfi", "nfi", D, D, F32)
        grs, b_grs = load_weight(ctx, "grs", "grs", D, 128, F32)
        ngis, b_ngis = load_weight(ctx, "ngis", "ngis", D, 128, F32)
        gis, b_gis = load_weight(ctx, "gis", "gis", D, 128, F32)
        m0d = nc.dram_tensor("m0s", [D, 128], F32, kind="ExternalOutput")

        ar, b_ar = ctx.sbuf("ar", [128, NCH * 128], F32)
        ai, b_ai = ctx.sbuf("ai", [128, NCH * 128], F32)
        mo, b_mo = ctx.sbuf("mo", [128, NCH * 128], F32)
        ps, b_ps = ctx.psum("m0ps", [128, 128], F32)

        def stage(pairs, dst, b_dst):
            for oc in range(NCH):
                total = len(pairs) * NCH
                k = 0
                for (lsb, b_l, ldout, rsb, b_r) in pairs:
                    for ic in range(NCH):
                        k += 1
                        is_last = (k == total)
                        p.op("te", lambda e, lsb=lsb, ldout=ldout, rsb=rsb, ic=ic, oc=oc,
                             s=(k == 1), t=is_last: e.matmul(
                                 ps[:, :], wblk(lsb, ic, oc, ldout),
                                 rsb[:, ic * 128:(ic + 1) * 128], start=s, stop=t),
                             reads=[b_l, b_r], writes=[b_ps], inc=(1 if is_last else 0))
                p.op("sc", lambda e, oc=oc, dst=dst: e.copy(dst[:, oc * 128:(oc + 1) * 128], ps[:, :]),
                     reads=[b_ps], writes=[b_dst])

        stage([(wrT, b_wrT, D, grs, b_grs), (wiT, b_wiT, D, ngis, b_ngis)], ar, b_ar)
        stage([(wrT, b_wrT, D, gis, b_gis), (wiT, b_wiT, D, grs, b_grs)], ai, b_ai)
        stage([(fr, b_fr, D, ar, b_ar), (nfi, b_nfi, D, ai, b_ai)], mo, b_mo)

        dst = bass.AP(m0d, 0, [[128, 128], [128 * 128, NCH], [1, 128]])
        p.op("sy", lambda e: e.dma_start(out=dst, in_=ap3(mo, 128, NCH, 0, 128)),
             reads=[b_mo], dma=True)
        p.finish()
        p.realize(nc)
    return nc


def build_ffn0():
    nc = bass.Bass("TRN2", target_bir_lowering=False)
    with contextlib.ExitStack() as st:
        ctx = Ctx(nc, st)
        p = ctx.p
        rin = nc.dram_tensor("r", [D, T], F32, kind="ExternalInput")
        rout = nc.dram_tensor("rout", [D, T], F32, kind="ExternalOutput")
        wm0, b_wm0 = load_weight(ctx, "wm0", "wm0", D, D, BF16)
        bl, b_bl = load_vec(ctx, "bias_l", "bias_l", NCH)

        ln = LN(ctx)
        Rt, b_Rt = ctx.sbuf("Rt", [128, NCH * TN], F32)
        xn, b_xn = ctx.sbuf("xn", [128, NCH * TN], BF16)
        negmu, b_negmu = ctx.sbuf("negmu", [128, TN], F32)
        av, b_av = ctx.sbuf("aval", [128, TN], F32)
        ev, b_ev = ctx.sbuf("ev", [128, TN], F32)
        pools = [ctx.psum(f"pp{i}", [128, TN], F32) for i in range(2)]

        for j in range(NT):
            col = j * TN
            p.op("sy", lambda e, col=col: e.dma_start(
                out=ap3(Rt, TN, NCH, 0, TN), in_=r_dram(rin, col, TN)),
                writes=[b_Rt], dma=True)
            ln.stats(Rt, b_Rt, TN, 0, TN, negmu, b_negmu, 0, av, b_av, 0)
            ln.apply(Rt, b_Rt, TN, 0, TN, negmu, b_negmu, 0, av, b_av, 0, xn, b_xn, TN, 0)

            def evac(oc, ps, b_ps):
                p.op("ve", lambda e, oc=oc, ps=ps: e.tensor_scalar(
                    ev[:, :], ps[:, 0:TN], bl[:, oc:oc + 1], 0.5, ALU.add, ALU.mult),
                    reads=[b_ps, b_bl], writes=[b_ev])
                p.op("ve", lambda e, oc=oc: e.tensor_tensor(
                    Rt[:, oc * TN:(oc + 1) * TN], Rt[:, oc * TN:(oc + 1) * TN],
                    ev[:, :], ALU.add),
                    reads=[b_ev], writes=[b_Rt])

            emit_proj(ctx, wm0, b_wm0, D, xn, b_xn, TN, 0, TN, pools, evac)
            p.op("sy", lambda e, col=col: e.dma_start(
                out=r_dram(rout, col, TN), in_=ap3(Rt, TN, NCH, 0, TN)),
                reads=[b_Rt], dma=True)
        p.finish()
        p.realize(nc)
    return nc


def build_attn0():
    nc = bass.Bass("TRN2", target_bir_lowering=False)
    with contextlib.ExitStack() as st:
        ctx = Ctx(nc, st)
        p = ctx.p
        rin = nc.dram_tensor("r", [D, T], F32, kind="ExternalInput")
        rout = nc.dram_tensor("rout", [D, T], F32, kind="ExternalOutput")
        wqkv, b_wqkv = load_weight(ctx, "wqkv", "wqkvT", D, 3 * D, BF16)
        wout, b_wout = load_weight(ctx, "wout", "woutT", D, D, BF16)
        bq, b_bq = load_vec(ctx, "bqkv_l", "bqkv_l", 3 * NCH)
        bo, b_bo = load_vec(ctx, "bout_l", "bout_l", NCH)
        idt, b_idt = load_weight(ctx, "ident", "ident", 128, 128, BF16)

        ln = LN(ctx)
        Rt, b_Rt = ctx.sbuf("Rt", [128, NCH * TN], F32)
        xn, b_xn = ctx.sbuf("xn", [128, NCH * TN], BF16)
        negmu, b_negmu = ctx.sbuf("negmu", [128, TN], F32)
        avl, b_avl = ctx.sbuf("aval", [128, TN], F32)
        qkv, b_qkv = ctx.sbuf("qkv", [128, 3 * NCH * TN], BF16)
        ssub, b_ssub = ctx.sbuf("ssub", [128, TN], F32)
        ebf, b_ebf = ctx.sbuf("ebf", [128, TN], BF16)
        abf, b_abf = ctx.sbuf("abf", [128, TN], BF16)
        mx, b_mx = ctx.sbuf("mx", [128, NT], F32)
        sm, b_sm = ctx.sbuf("sm", [128, NT], F32)
        rs, b_rs = ctx.sbuf("rsum", [128, NT], F32)
        atT, b_atT = ctx.sbuf("atT", [64, 1024], BF16)
        vT, b_vT = ctx.sbuf("vT", [64, 1024], BF16)
        avo, b_avo = ctx.sbuf("avo", [128, NCH * TN], BF16)
        ev, b_ev = ctx.sbuf("ev", [128, TN], F32)
        pools = [ctx.psum(f"pp{i}", [128, TN], F32) for i in range(2)]
        psS, b_psS = ctx.psum("att_psS", [128, TN], F32)
        psA, b_psA = ctx.psum("att_psA", [128, TN], F32)
        psT1, b_psT1 = ctx.psum("att_psT1", [64, 1024], BF16)
        psT2, b_psT2 = ctx.psum("att_psT2", [64, 1024], BF16)

        for j in range(NT):
            col = j * TN
            p.op("sy", lambda e, col=col: e.dma_start(
                out=ap3(Rt, TN, NCH, 0, TN), in_=r_dram(rin, col, TN)),
                writes=[b_Rt], dma=True)
            ln.stats(Rt, b_Rt, TN, 0, TN, negmu, b_negmu, 0, avl, b_avl, 0)
            ln.apply(Rt, b_Rt, TN, 0, TN, negmu, b_negmu, 0, avl, b_avl, 0, xn, b_xn, TN, 0)

            def evq(oc, ps, b_ps):
                sc = 0.125 if oc < NCH else 1.0
                p.op("ve", lambda e, oc=oc, ps=ps, sc=sc: e.tensor_scalar(
                    qkv[:, oc * TN:(oc + 1) * TN], ps[:, 0:TN],
                    bq[:, oc:oc + 1], sc, ALU.add, ALU.mult),
                    reads=[b_ps, b_bq], writes=[b_qkv])

            emit_proj(ctx, wqkv, b_wqkv, 3 * D, xn, b_xn, TN, 0, TN, pools, evq)

            for cch in range(NCH):
                qc = cch * TN
                kc = (NCH + cch) * TN
                vc = (2 * NCH + cch) * TN
                for blk in range(8):
                    o = blk * 64
                    p.op("te", lambda e, qc=qc, kc=kc, o=o: e.matmul(
                        psS[0:64, o:o + 64], qkv[0:64, qc + o:qc + o + 64],
                        qkv[0:64, kc + o:kc + o + 64],
                        start=True, stop=True, tile_position=(0, 0)),
                        reads=[b_qkv], writes=[b_psS])
                    p.op("te", lambda e, qc=qc, kc=kc, o=o: e.matmul(
                        psS[64:128, o:o + 64], qkv[64:128, qc + o:qc + o + 64],
                        qkv[64:128, kc + o:kc + o + 64],
                        start=True, stop=True, tile_position=(64, 64)),
                        reads=[b_qkv], writes=[b_psS])
                ps3 = psS.ap().rearrange("p (b n) -> p b n", n=64)
                p.op("ve", lambda e, ps3=ps3: e.tensor_reduce(mx[:, :], ps3, AX.X, ALU.max),
                     reads=[b_psS], writes=[b_mx])
                p.op("ve", lambda e, ps3=ps3: e.tensor_tensor(
                    ssub.ap().rearrange("p (b n) -> p b n", n=64), ps3, bci(mx, 0, 8),
                    ALU.subtract),
                    reads=[b_psS, b_mx], writes=[b_ssub])
                p.op("sc", lambda e: e.activation(ebf[:, :], ssub[:, :], AF.Exp),
                     reads=[b_ssub], writes=[b_ebf])
                p.op("ve", lambda e: e.tensor_reduce(
                    sm[:, :], ebf.ap().rearrange("p (b n) -> p b n", n=64), AX.X, ALU.add),
                    reads=[b_ebf], writes=[b_sm])
                p.op("ve", lambda e: e.reciprocal(rs[:, :], sm[:, :]), reads=[b_sm], writes=[b_rs])
                p.op("ve", lambda e: e.tensor_tensor(
                    abf.ap().rearrange("p (b n) -> p b n", n=64),
                    ebf.ap().rearrange("p (b n) -> p b n", n=64), bci(rs, 0, 8), ALU.mult),
                    reads=[b_ebf, b_rs], writes=[b_abf])
                for blk in range(8):
                    o = blk * 64
                    p.op("te", lambda e, o=o, blk=blk: e.transpose(
                        psT1[:, blk * 128:(blk + 1) * 128], abf[:, o:o + 64], idt[:, :]),
                        reads=[b_abf, b_idt], writes=[b_psT1])
                    p.op("te", lambda e, o=o, blk=blk, vc=vc: e.transpose(
                        psT2[:, blk * 128:(blk + 1) * 128], qkv[:, vc + o:vc + o + 64], idt[:, :]),
                        reads=[b_qkv, b_idt], writes=[b_psT2])
                p.op("sc", lambda e: e.copy(atT[:, :], psT1[:, :]), reads=[b_psT1], writes=[b_atT])
                p.op("sc", lambda e: e.copy(vT[:, :], psT2[:, :]), reads=[b_psT2], writes=[b_vT])
                for blk in range(8):
                    o = blk * 64
                    p.op("te", lambda e, blk=blk, o=o: e.matmul(
                        psA[0:64, o:o + 64], vT[0:64, blk * 128:blk * 128 + 64],
                        atT[0:64, blk * 128:blk * 128 + 64],
                        start=True, stop=True, tile_position=(0, 0)),
                        reads=[b_vT, b_atT], writes=[b_psA])
                    p.op("te", lambda e, blk=blk, o=o: e.matmul(
                        psA[64:128, o:o + 64], vT[0:64, blk * 128 + 64:blk * 128 + 128],
                        atT[0:64, blk * 128 + 64:blk * 128 + 128],
                        start=True, stop=True, tile_position=(0, 64)),
                        reads=[b_vT, b_atT], writes=[b_psA])
                p.op("sc", lambda e, cch=cch: e.copy(avo[:, cch * TN:(cch + 1) * TN], psA[:, :]),
                     reads=[b_psA], writes=[b_avo])

            def evo(oc, ps, b_ps):
                p.op("ve", lambda e, oc=oc, ps=ps: e.tensor_scalar(
                    ev[:, :], ps[:, 0:TN], bo[:, oc:oc + 1], None, ALU.add),
                    reads=[b_ps, b_bo], writes=[b_ev])
                p.op("ve", lambda e, oc=oc: e.tensor_tensor(
                    Rt[:, oc * TN:(oc + 1) * TN], Rt[:, oc * TN:(oc + 1) * TN],
                    ev[:, :], ALU.add),
                    reads=[b_ev], writes=[b_Rt])

            emit_proj(ctx, wout, b_wout, D, avo, b_avo, TN, 0, TN, pools, evo)
            p.op("sy", lambda e, col=col: e.dma_start(
                out=r_dram(rout, col, TN), in_=ap3(Rt, TN, NCH, 0, TN)),
                reads=[b_Rt], dma=True)
        p.finish()
        p.realize(nc)
    return nc


def build_ffn1():
    nc = bass.Bass("TRN2", target_bir_lowering=False)
    with contextlib.ExitStack() as st:
        ctx = Ctx(nc, st)
        p = ctx.p
        rin = nc.dram_tensor("r", [D, T], F32, kind="ExternalInput")
        rout = nc.dram_tensor("rout", [D, T], F32, kind="ExternalOutput")
        halo = nc.dram_tensor("halo", [128, NCH], F32, kind="ExternalInput")
        wr, b_wr = load_weight(ctx, "wr1", "wrT1", D, D, BF16)
        wk, b_wk = load_weight(ctx, "wk1", "wkT1", D, D, BF16)
        wv, b_wv = load_weight(ctx, "wv1", "wvT1", D, D, BF16)
        mur, b_mur = load_vec(ctx, "mur_l", "mur_l", NCH)
        muk, b_muk = load_vec(ctx, "muk_l", "muk_l", NCH)

        ln = LN(ctx)
        Rt, b_Rt = ctx.sbuf("Rt", [128, NCH * TN], F32)
        negmu, b_negmu = ctx.sbuf("negmu", [128, T], F32)
        avl, b_avl = ctx.sbuf("aval", [128, T], F32)
        hmu, b_hmu = ctx.sbuf("hmu", [128, 1], F32)
        ha, b_ha = ctx.sbuf("ha", [128, 1], F32)
        ht, b_ht = ctx.sbuf("ht", [128, NCH], F32)
        hxn, b_hxn = ctx.sbuf("hxn", [128, NCH], BF16)
        XW = TN + 1
        xe = [ctx.sbuf(f"xe{i}", [128, NCH * XW], BF16) for i in range(2)]
        dt, b_dt = ctx.sbuf("dt", [128, NCH * TN], BF16)
        gx, b_gx = ctx.sbuf("gx", [128, NCH * TN], BF16)
        xr, b_xr = ctx.sbuf("xr", [128, NCH * TN], BF16)
        xk, b_xk = ctx.sbuf("xk", [128, NCH * TN], BF16)
        rg, b_rg = ctx.sbuf("rg", [128, NCH * TN], F32)
        krel, b_krel = ctx.sbuf("krel", [128, TN], F32)
        kact, b_kact = ctx.sbuf("kact", [128, NCH * TN], BF16)
        tv, b_tv = ctx.sbuf("tv", [128, TN], F32)
        pools = [ctx.psum(f"pp{i}", [128, TN], F32) for i in range(2)]

        p.op("sy", lambda e: e.dma_start(out=ht[:, :], in_=halo[:, :]),
             writes=[b_ht], dma=True)

        for j in range(NT):
            col = j * TN
            p.op("sy", lambda e, col=col: e.dma_start(
                out=ap3(Rt, TN, NCH, 0, TN), in_=r_dram(rin, col, TN)),
                writes=[b_Rt], dma=True)
            ln.stats(Rt, b_Rt, TN, 0, TN, negmu, b_negmu, col, avl, b_avl, col)
        ln.stats(ht, b_ht, 1, 0, 1, hmu, b_hmu, 0, ha, b_ha, 0)
        ln.apply(ht, b_ht, 1, 0, 1, hmu, b_hmu, 0, ha, b_ha, 0, hxn, b_hxn, 1, 0)

        for j in range(NT):
            col = j * TN
            xei, b_xei = xe[j % 2]
            xprev, b_xprev = xe[(j + 1) % 2]
            p.op("sy", lambda e, col=col: e.dma_start(
                out=ap3(Rt, TN, NCH, 0, TN), in_=r_dram(rin, col, TN)),
                writes=[b_Rt], dma=True)
            if j == 0:
                p.op("ve", lambda e, xei=xei: e.tensor_copy(
                    ap3(xei, XW, NCH, 0, 1), ap3(hxn, 1, NCH, 0, 1)),
                    reads=[b_hxn], writes=[b_xei])
            else:
                p.op("ve", lambda e, xei=xei, xprev=xprev: e.tensor_copy(
                    ap3(xei, XW, NCH, 0, 1), ap3(xprev, XW, NCH, TN, 1)),
                    reads=[b_xprev], writes=[b_xei])
            ln.apply(Rt, b_Rt, TN, 0, TN, negmu, b_negmu, col, avl, b_avl, col,
                     xei, b_xei, XW, 1)
            cur = ap3(xei, XW, NCH, 1, TN)
            prv = ap3(xei, XW, NCH, 0, TN)
            p.op("ve", lambda e, cur=cur, prv=prv: e.tensor_tensor(
                ap3(dt, TN, NCH, 0, TN), cur, prv, ALU.subtract),
                reads=[b_xei], writes=[b_dt])
            for (mu, b_mu, xdst, b_xdst) in ((mur, b_mur, xr, b_xr), (muk, b_muk, xk, b_xk)):
                for ic in range(NCH):
                    p.op("ve", lambda e, ic=ic, mu=mu: e.tensor_scalar(
                        gx[:, ic * TN:(ic + 1) * TN], dt[:, ic * TN:(ic + 1) * TN],
                        mu[:, ic:ic + 1], None, ALU.mult),
                        reads=[b_dt, b_mu], writes=[b_gx])
                p.op("ve", lambda e, xdst=xdst, prv=prv: e.tensor_tensor(
                    ap3(xdst, TN, NCH, 0, TN), ap3(gx, TN, NCH, 0, TN), prv, ALU.add),
                    reads=[b_gx, b_xei], writes=[b_xdst])

            def ev_r(oc, ps, b_ps):
                p.op("sc", lambda e, oc=oc, ps=ps: e.activation(
                    rg[:, oc * TN:(oc + 1) * TN], ps[:, 0:TN], AF.Sigmoid),
                    reads=[b_ps], writes=[b_rg])

            emit_proj(ctx, wr, b_wr, D, xr, b_xr, TN, 0, TN, pools, ev_r)

            def ev_k(oc, ps, b_ps):
                p.op("ve", lambda e, ps=ps: e.tensor_scalar(
                    krel[:, :], ps[:, 0:TN], 0.0, None, ALU.max),
                    reads=[b_ps], writes=[b_krel])
                p.op("sc", lambda e, oc=oc: e.activation(
                    kact[:, oc * TN:(oc + 1) * TN], krel[:, :], AF.Square),
                    reads=[b_krel], writes=[b_kact])

            emit_proj(ctx, wk, b_wk, D, xk, b_xk, TN, 0, TN, pools, ev_k)

            def ev_v(oc, ps, b_ps):
                p.op("ve", lambda e, oc=oc, ps=ps: e.tensor_tensor(
                    tv[:, :], ps[:, 0:TN], rg[:, oc * TN:(oc + 1) * TN], ALU.mult),
                    reads=[b_ps, b_rg], writes=[b_tv])
                p.op("ve", lambda e, oc=oc: e.scalar_tensor_tensor(
                    Rt[:, oc * TN:(oc + 1) * TN], tv[:, :], 0.5,
                    Rt[:, oc * TN:(oc + 1) * TN], ALU.mult, ALU.add),
                    reads=[b_tv], writes=[b_Rt])

            emit_proj(ctx, wv, b_wv, D, kact, b_kact, TN, 0, TN, pools, ev_v)
            p.op("sy", lambda e, col=col: e.dma_start(
                out=r_dram(rout, col, TN), in_=ap3(Rt, TN, NCH, 0, TN)),
                reads=[b_Rt], dma=True)
        p.finish()
        p.realize(nc)
    return nc


def build_linattn_m():
    nc = bass.Bass("TRN2", target_bir_lowering=False)
    with contextlib.ExitStack() as st:
        ctx = Ctx(nc, st)
        p = ctx.p
        rin = nc.dram_tensor("r", [D, T], F32, kind="ExternalInput")
        mout = nc.dram_tensor("mpart", [128, D], F32, kind="ExternalOutput")
        wk, b_wk = load_weight(ctx, "wk", "wkT", D, 128, BF16)
        idt, b_idt = load_weight(ctx, "ident", "ident", 128, 128, BF16)

        ln = LN(ctx)
        Rt, b_Rt = ctx.sbuf("Rt", [128, NCH * TN], F32)
        xn, b_xn = ctx.sbuf("xn", [128, NCH * TN], BF16)
        negmu, b_negmu = ctx.sbuf("negmu", [128, TN], F32)
        avl, b_avl = ctx.sbuf("aval", [128, TN], F32)
        ktok, b_ktok = ctx.sbuf("ktok", [128, 128], BF16)
        jtok, b_jtok = ctx.sbuf("jtok", [128, 1024], BF16)
        msb, b_msb = ctx.sbuf("msb", [128, 1024], F32)
        psK, b_psK = ctx.psum("la_psK", [128, 128], F32)
        psJ, b_psJ = ctx.psum("la_psJ", [128, 1024], BF16)
        psM = [ctx.psum(f"la_psM{i}", [128, TN], F32) for i in range(2)]

        nblk = T // 128
        for j in range(NT):
            col = j * TN
            p.op("sy", lambda e, col=col: e.dma_start(
                out=ap3(Rt, TN, NCH, 0, TN), in_=r_dram(rin, col, TN)),
                writes=[b_Rt], dma=True)
            ln.stats(Rt, b_Rt, TN, 0, TN, negmu, b_negmu, 0, avl, b_avl, 0)
            ln.apply(Rt, b_Rt, TN, 0, TN, negmu, b_negmu, 0, avl, b_avl, 0, xn, b_xn, TN, 0)
            for tb in range(TN // 128):
                gi = j * (TN // 128) + tb
                for ic in range(NCH):
                    p.op("te", lambda e, ic=ic, tb=tb: e.matmul(
                        psK[:, :], xn[:, ic * TN + tb * 128: ic * TN + (tb + 1) * 128],
                        wk[:, ic * 128:(ic + 1) * 128],
                        start=(ic == 0), stop=(ic == NCH - 1)),
                        reads=[b_xn, b_wk], writes=[b_psK],
                        inc=(1 if ic == NCH - 1 else 0))
                p.op("sc", lambda e: e.copy(ktok[:, :], psK[:, :]), reads=[b_psK], writes=[b_ktok])
                for ic in range(NCH):
                    p.op("te", lambda e, ic=ic, tb=tb: e.transpose(
                        psJ[:, ic * 128:(ic + 1) * 128],
                        xn[:, ic * TN + tb * 128: ic * TN + (tb + 1) * 128], idt[:, :]),
                        reads=[b_xn, b_idt], writes=[b_psJ])
                p.op("sc", lambda e: e.copy(jtok[:, :], psJ[:, :]), reads=[b_psJ], writes=[b_jtok])
                for half in range(2):
                    ps, b_ps = psM[half]
                    p.op("te", lambda e, ps=ps, half=half, s=(gi == 0), t=(gi == nblk - 1): e.matmul(
                        ps[:, :], ktok[:, :], jtok[:, half * TN:(half + 1) * TN],
                        start=s, stop=t),
                        reads=[b_ktok, b_jtok], writes=[b_ps])
        for half in range(2):
            ps, b_ps = psM[half]
            p.op("sc", lambda e, half=half, ps=ps: e.copy(
                msb[:, half * TN:(half + 1) * TN], ps[:, :]),
                reads=[b_ps], writes=[b_msb])
        p.op("sy", lambda e: e.dma_start(out=mout[:, :], in_=msb[:, :]),
             reads=[b_msb], dma=True)
        p.finish()
        p.realize(nc)
    return nc


def build_linattn_apply():
    nc = bass.Bass("TRN2", target_bir_lowering=False)
    with contextlib.ExitStack() as st:
        ctx = Ctx(nc, st)
        p = ctx.p
        rin = nc.dram_tensor("r", [D, T], F32, kind="ExternalInput")
        rout = nc.dram_tensor("rout", [D, T], F32, kind="ExternalOutput")
        mp = nc.dram_tensor("mpair", [2 * 128, D], F32, kind="ExternalInput")
        wq, b_wq = load_weight(ctx, "wq", "wqT", D, 128, BF16)
        gam, b_gam = load_vec(ctx, "gam_l", "gam_l", NCH)
        bet, b_bet = load_vec(ctx, "bet_l", "bet_l", NCH)

        m0sb, b_m0sb = ctx.sbuf("m0sb", [128, D], F32)
        m1sb, b_m1sb = ctx.sbuf("m1sb", [128, D], F32)
        mbf, b_mbf = ctx.sbuf("mbf", [128, D], BF16)
        p.op("sy", lambda e: e.dma_start(out=m0sb[:, :], in_=mp[0:128, :]),
             writes=[b_m0sb], dma=True)
        p.op("sy", lambda e: e.dma_start(out=m1sb[:, :], in_=mp[128:256, :]),
             writes=[b_m1sb], dma=True)
        p.op("ve", lambda e: e.tensor_tensor(mbf[:, :], m0sb[:, :], m1sb[:, :], ALU.add),
             reads=[b_m0sb, b_m1sb], writes=[b_mbf])

        ln = LN(ctx)
        Rt, b_Rt = ctx.sbuf("Rt", [128, NCH * TN], F32)
        xn, b_xn = ctx.sbuf("xn", [128, NCH * TN], BF16)
        negmu, b_negmu = ctx.sbuf("negmu", [128, TN], F32)
        avl, b_avl = ctx.sbuf("aval", [128, TN], F32)
        qbf, b_qbf = ctx.sbuf("qbf", [128, TN], BF16)
        jn, b_jn = ctx.sbuf("jn", [128, NCH * TN], F32)
        nmu2, b_nmu2 = ctx.sbuf("nmu2", [128, TN], F32)
        av2, b_av2 = ctx.sbuf("av2", [128, TN], F32)
        g1, b_g1 = ctx.sbuf("g1", [128, TN], F32)
        pools = [ctx.psum(f"pp{i}", [128, TN], F32) for i in range(2)]

        for j in range(NT):
            col = j * TN
            p.op("sy", lambda e, col=col: e.dma_start(
                out=ap3(Rt, TN, NCH, 0, TN), in_=r_dram(rin, col, TN)),
                writes=[b_Rt], dma=True)
            ln.stats(Rt, b_Rt, TN, 0, TN, negmu, b_negmu, 0, avl, b_avl, 0)
            ln.apply(Rt, b_Rt, TN, 0, TN, negmu, b_negmu, 0, avl, b_avl, 0, xn, b_xn, TN, 0)

            def evq(oc, ps, b_ps):
                p.op("sc", lambda e, ps=ps: e.copy(qbf[:, :], ps[:, 0:TN]),
                     reads=[b_ps], writes=[b_qbf])

            emit_proj(ctx, wq, b_wq, 128, xn, b_xn, TN, 0, TN, pools, evq)

            for dc in range(NCH):
                ps, b_ps = pools[dc % 2]
                p.op("te", lambda e, dc=dc, ps=ps: e.matmul(
                    ps[:, 0:TN], mbf[:, dc * 128:(dc + 1) * 128], qbf[:, :],
                    start=True, stop=True),
                    reads=[b_mbf, b_qbf], writes=[b_ps])
                p.op("ve", lambda e, dc=dc, ps=ps: e.scalar_tensor_tensor(
                    jn[:, dc * TN:(dc + 1) * TN], ps[:, 0:TN], RSQ_DK,
                    xn[:, dc * TN:(dc + 1) * TN], ALU.mult, ALU.add),
                    reads=[b_ps, b_xn], writes=[b_jn])

            ln.stats(jn, b_jn, TN, 0, TN, nmu2, b_nmu2, 0, av2, b_av2, 0)
            p.op("ve", lambda e: e.tensor_tensor(
                ap3(ln.scr, TN, NCH, 0, TN), ap3(jn, TN, NCH, 0, TN),
                bc3(nmu2, 0, TN, NCH), ALU.add),
                reads=[b_jn, b_nmu2], writes=[ln.b_scr])
            p.op("ve", lambda e: e.tensor_tensor(
                ap3(ln.scr, TN, NCH, 0, TN), ap3(ln.scr, TN, NCH, 0, TN),
                bc3(av2, 0, TN, NCH), ALU.mult),
                reads=[b_av2], writes=[ln.b_scr])
            for ic in range(NCH):
                p.op("ve", lambda e, ic=ic: e.tensor_scalar(
                    g1[:, :], ln.scr[:, ic * TN:(ic + 1) * TN],
                    gam[:, ic:ic + 1], bet[:, ic:ic + 1], ALU.mult, ALU.add),
                    reads=[ln.b_scr, b_gam, b_bet], writes=[b_g1])
                p.op("ve", lambda e, ic=ic: e.tensor_tensor(
                    Rt[:, ic * TN:(ic + 1) * TN], Rt[:, ic * TN:(ic + 1) * TN],
                    g1[:, :], ALU.add),
                    reads=[b_g1], writes=[b_Rt])
            p.op("sy", lambda e, col=col: e.dma_start(
                out=r_dram(rout, col, TN), in_=ap3(Rt, TN, NCH, 0, TN)),
                reads=[b_Rt], dma=True)
        p.finish()
        p.realize(nc)
    return nc



def _dft_constants():
    dd = np.arange(D, dtype=np.float64)
    ang = -2.0 * np.pi * np.outer(dd, dd) / D
    Fr = np.cos(ang).astype(np.float32)
    Fi = np.sin(ang).astype(np.float32)
    Gr = (np.cos(ang) / D).astype(np.float32)
    Gi = (-np.sin(ang) / D).astype(np.float32)
    return Fr, Fi, Gr, Gi


def _vec_l(v):
    n = v.shape[0] // 128
    return np.ascontiguousarray(v.reshape(n, 128).T.astype(np.float32))


def kernel(**inputs):
    return kernel_fused(**inputs)


def kernel_unfused(L0_attn_x, L0_attn_W_qkv, L0_attn_b_qkv, L0_attn_W_out, L0_attn_b_out,
           L0_ffn_w_freq_real, L0_ffn_w_freq_imag, L0_ffn_bias,
           L1_attn_W_Q, L1_attn_W_K, L1_attn_gamma, L1_attn_beta,
           L1_ffn_W_r, L1_ffn_W_k, L1_ffn_W_v, L1_ffn_mu_r, L1_ffn_mu_k):
    NCORES = 8
    cores = list(range(NCORES))
    x = np.asarray(L0_attn_x, np.float32)

    shards = []
    for c in cores:
        b, h = c // 2, c % 2
        shards.append(np.ascontiguousarray(x[b, h * T:(h + 1) * T, :].T))

    Fr, Fi, Gr, Gi = _dft_constants()
    wrT = np.ascontiguousarray(np.asarray(L0_ffn_w_freq_real, np.float32).T)
    wiT = np.ascontiguousarray(np.asarray(L0_ffn_w_freq_imag, np.float32).T)
    nfi = np.ascontiguousarray(-Fi)

    ncA = build_m0()
    in_maps = []
    for c in cores:
        sl = slice(c * 128, (c + 1) * 128)
        in_maps.append({
            "wrT": wrT, "wiT": wiT, "fr": Fr, "nfi": nfi,
            "grs": np.ascontiguousarray(Gr[:, sl]),
            "ngis": np.ascontiguousarray(-Gi[:, sl]),
            "gis": np.ascontiguousarray(Gi[:, sl]),
        })
    resA = run_bass_kernel_spmd(ncA, in_maps, core_ids=cores)
    m0 = np.concatenate([resA.results[c]["m0s"] for c in cores], axis=1)

    ident = np.eye(128, dtype=np.float32)

    ncB = build_ffn0()
    bias_l = _vec_l(np.asarray(L0_ffn_bias, np.float32))
    resB = run_bass_kernel_spmd(
        ncB, [{"r": shards[c], "wm0": m0, "bias_l": bias_l} for c in cores], core_ids=cores)
    r1 = [resB.results[c]["rout"] for c in cores]

    ncC = build_attn0()
    wqkvT = np.ascontiguousarray(np.asarray(L0_attn_W_qkv, np.float32).T)
    woutT = np.ascontiguousarray(np.asarray(L0_attn_W_out, np.float32).T)
    bqkv_l = _vec_l(np.asarray(L0_attn_b_qkv, np.float32))
    bout_l = _vec_l(np.asarray(L0_attn_b_out, np.float32))
    resC = run_bass_kernel_spmd(
        ncC, [{"r": r1[c], "wqkvT": wqkvT, "bqkv_l": bqkv_l, "woutT": woutT,
               "bout_l": bout_l, "ident": ident} for c in cores], core_ids=cores)
    r2 = [resC.results[c]["rout"] for c in cores]

    resD = run_bass_kernel_spmd(
        ncB, [{"r": r2[c], "wm0": m0, "bias_l": bias_l} for c in cores], core_ids=cores)
    r3 = [resD.results[c]["rout"] for c in cores]

    ncE = build_ffn1()
    wrT1 = np.ascontiguousarray(np.asarray(L1_ffn_W_r, np.float32).T)
    wkT1 = np.ascontiguousarray(np.asarray(L1_ffn_W_k, np.float32).T)
    wvT1 = np.ascontiguousarray(np.asarray(L1_ffn_W_v, np.float32).T)
    mur_l = _vec_l(np.asarray(L1_ffn_mu_r, np.float32))
    muk_l = _vec_l(np.asarray(L1_ffn_mu_k, np.float32))

    def halos(rlist):
        out = []
        for c in cores:
            if c % 2 == 0:
                out.append(np.zeros((128, NCH), np.float32))
            else:
                out.append(_vec_l(rlist[c - 1][:, T - 1]))
        return out

    h3 = halos(r3)
    resE = run_bass_kernel_spmd(
        ncE, [{"r": r3[c], "wrT1": wrT1, "wkT1": wkT1, "wvT1": wvT1,
               "mur_l": mur_l, "muk_l": muk_l, "halo": h3[c]} for c in cores],
        core_ids=cores)
    r4 = [resE.results[c]["rout"] for c in cores]

    ncF = build_linattn_m()
    wkT = np.ascontiguousarray(np.asarray(L1_attn_W_K, np.float32).T)
    resF = run_bass_kernel_spmd(
        ncF, [{"r": r4[c], "wkT": wkT, "ident": ident} for c in cores], core_ids=cores)
    mparts = [resF.results[c]["mpart"] for c in cores]

    ncG = build_linattn_apply()
    wqT = np.ascontiguousarray(np.asarray(L1_attn_W_Q, np.float32).T)
    gam_l = _vec_l(np.asarray(L1_attn_gamma, np.float32))
    bet_l = _vec_l(np.asarray(L1_attn_beta, np.float32))
    resG = run_bass_kernel_spmd(
        ncG, [{"r": r4[c],
               "mpair": np.concatenate([mparts[(c // 2) * 2], mparts[(c // 2) * 2 + 1]], axis=0),
               "wqT": wqT, "gam_l": gam_l, "bet_l": bet_l} for c in cores],
        core_ids=cores)
    r5 = [resG.results[c]["rout"] for c in cores]

    h5 = halos(r5)
    resH = run_bass_kernel_spmd(
        ncE, [{"r": r5[c], "wrT1": wrT1, "wkT1": wkT1, "wvT1": wvT1,
               "mur_l": mur_l, "muk_l": muk_l, "halo": h5[c]} for c in cores],
        core_ids=cores)
    r6 = [resH.results[c]["rout"] for c in cores]

    out = np.empty_like(x)
    for c in cores:
        b, h = c // 2, c % 2
        out[b, h * T:(h + 1) * T, :] = r6[c].T
    return out



MAGIC = 777


class PairComm:

    def __init__(self, ctx, st):
        nc, p = ctx.nc, ctx.p
        self.ctx = ctx
        self.fl = nc.dram_tensor("flags", [2, 8], mybir.dt.int32, addr_space="Shared")
        self.regs = {r: st.enter_context(nc.gpsimd.register("rg_" + r))
                     for r in ("ms", "fv", "cond", "bk")}
        self.mrow, self.b_mrow = ctx.sbuf("cc_mrow", [1, 1], mybir.dt.int32)
        self.zrow, self.b_zrow = ctx.sbuf("cc_zrow", [1, 1], mybir.dt.int32)
        self.cid = nc.dram_tensor("cid", [1, 1], mybir.dt.int32, kind="ExternalInput")
        self.cids, self.b_cids = ctx.sbuf("cc_cids", [1, 1], mybir.dt.int32)
        p.op("gp", lambda e: e.memset(self.mrow[:, :], MAGIC), writes=[self.b_mrow])
        p.op("gp", lambda e: e.memset(self.zrow[:, :], 0), writes=[self.b_zrow])
        p.op("gp", lambda e: e.dma_start(out=self.cids[:, :], in_=self.cid[:, :]),
             writes=[self.b_cids], dma=True)
        p.op("gp", lambda e: e.reg_load(self.regs["ms"], self.cids[:1, :1]),
             reads=[self.b_cids])

    def send(self, data_emit, b_datas, idx):
        p = self.ctx.p
        r = self.regs

        after_data = self.ctx.p.cnt.get("dgp", 0) + 16

        def body(e, sems, after_data=after_data):
            with e.If_eq(r["ms"], 0):
                data_emit(e, 0).then_inc(sems["dgp"], 16)
            with e.Else():
                data_emit(e, 1).then_inc(sems["dgp"], 16)
            e.wait_ge(sems["dgp"], after_data)
            with e.If_eq(r["ms"], 0):
                e.dma_start(out=self.fl[0:1, idx:idx + 1],
                            in_=self.mrow[:, :]).then_inc(sems["dgp"], 16)
            with e.Else():
                e.dma_start(out=self.fl[1:2, idx:idx + 1],
                            in_=self.mrow[:, :]).then_inc(sems["dgp"], 16)
        body.needs_sems = True
        tok = p.op("gp", body, reads=list(b_datas) + [self.b_mrow], dma=True, inc=32)
        return tok

    def wait_partner(self, idx):
        p = self.ctx.p
        r = self.regs

        def body(e, sems):
            with e.If_eq(r["ms"], 0):
                e.reg_mov(r["cond"], 1)
                with e.While(r["cond"]):
                    for _ in range(256):
                        e.reg_alu(r["bk"], r["bk"], 1, ALU.add)
                    e.load(r["fv"], self.fl[1:2, idx:idx + 1])
                    e.reg_alu(r["cond"], r["fv"], MAGIC, ALU.not_equal)
                e.dma_start(out=self.fl[1:2, idx:idx + 1],
                            in_=self.zrow[:, :]).then_inc(sems["dgp"], 16)
            with e.Else():
                e.reg_mov(r["cond"], 1)
                with e.While(r["cond"]):
                    for _ in range(256):
                        e.reg_alu(r["bk"], r["bk"], 1, ALU.add)
                    e.load(r["fv"], self.fl[0:1, idx:idx + 1])
                    e.reg_alu(r["cond"], r["fv"], MAGIC, ALU.not_equal)
                e.dma_start(out=self.fl[0:1, idx:idx + 1],
                            in_=self.zrow[:, :]).then_inc(sems["dgp"], 16)
        body.needs_sems = True
        return p.op("gp", body, reads=[self.b_zrow], dma=True, inc=16)


def fload_weight(ctx, name, dr, din, dout, dtype=BF16):
    p = ctx.p
    sb, buf = ctx.sbuf(name, [128, (din // 128) * dout], dtype)
    src = bass.AP(dr, 0, [[dout, 128], [128 * dout, din // 128], [1, dout]])
    dst = ap3(sb, dout, din // 128, 0, dout)
    eng = "gp" if dtype != F32 else "sy"
    p.op(eng, lambda e, d=dst, s=src: e.dma_start(out=d, in_=s), writes=[buf], dma=True)
    return sb, buf


def fload_vec(ctx, name, dr, n):
    p = ctx.p
    sb, buf = ctx.sbuf(name, [128, n], F32)
    p.op("sy", lambda e, d=sb, s=dr: e.dma_start(out=d[:, :], in_=s[:, :]),
         writes=[buf], dma=True)
    return sb, buf


def emit_m0_phase(ctx, dr, comm):
    nc, p = ctx.nc, ctx.p
    HN = 512
    wrT, b_wrT = fload_weight(ctx, "f_wrT", dr["wrT"], D, D, F32R)
    wiT, b_wiT = fload_weight(ctx, "f_wiT", dr["wiT"], D, D, F32R)
    grs, b_grs = fload_weight(ctx, "f_grs", dr["grh"], D, HN, F32R)
    ngis, b_ngis = fload_weight(ctx, "f_ngis", dr["ngih"], D, HN, F32R)
    gis, b_gis = fload_weight(ctx, "f_gis", dr["gih"], D, HN, F32R)
    mo, b_mo = ctx.sbuf("f_mo", [128, NCH * HN], F32)
    arr, b_arr = ctx.sbuf("f_arr", [128, NCH * HN], F32R)
    air, b_air = ctx.sbuf("f_air", [128, NCH * HN], F32R)
    ps, b_ps = ctx.psum("f_m0ps", [128, HN], F32)
    fring = [(ctx.sbuf(f"f_fr{i}", [128, NCH * 128], F32R),
              ctx.sbuf(f"f_nfi{i}", [128, NCH * 128], F32R)) for i in range(2)]

    def stage1(dst, b_dst, r1, b_r1, r2, b_r2):
        for oc in range(NCH):
            k = 0
            for (lsb, b_l, rsb, b_r) in ((wrT, b_wrT, r1, b_r1), (wiT, b_wiT, r2, b_r2)):
                for ic in range(NCH):
                    k += 1
                    last = (k == 16)
                    p.op("te", lambda e, lsb=lsb, rsb=rsb, ic=ic, oc=oc,
                         s=(k == 1), t=last: e.matmul(
                             ps[:, :], wblk(lsb, ic, oc, D),
                             rsb[:, ic * HN:(ic + 1) * HN], start=s, stop=t),
                         reads=[b_l, b_r], writes=[b_ps], inc=(1 if last else 0))
            p.op("sc", lambda e, oc=oc, dst=dst: e.copy(
                dst[:, oc * HN:(oc + 1) * HN], ps[:, :]),
                reads=[b_ps], writes=[b_dst])

    stage1(arr, b_arr, grs, b_grs, ngis, b_ngis)
    stage1(air, b_air, gis, b_gis, grs, b_grs)

    for oc in range(NCH):
        (frs, b_frs), (nfis, b_nfis) = fring[oc % 2]
        src_fr = bass.AP(dr["fr"], oc * 128, [[D, 128], [128 * D, NCH], [1, 128]])
        src_nfi = bass.AP(dr["nfi"], oc * 128, [[D, 128], [128 * D, NCH], [1, 128]])
        p.op("gp", lambda e, s=src_fr, d=frs: e.dma_start(
            out=ap3(d, 128, NCH, 0, 128), in_=s), writes=[b_frs], dma=True)
        p.op("gp", lambda e, s=src_nfi, d=nfis: e.dma_start(
            out=ap3(d, 128, NCH, 0, 128), in_=s), writes=[b_nfis], dma=True)
        k = 0
        for (lsb, b_l, rsb, b_r) in ((frs, b_frs, arr, b_arr), (nfis, b_nfis, air, b_air)):
            for mc in range(NCH):
                k += 1
                last = (k == 16)
                p.op("te", lambda e, lsb=lsb, rsb=rsb, mc=mc, s=(k == 1), t=last: e.matmul(
                    ps[:, :], lsb[:, mc * 128:(mc + 1) * 128],
                    rsb[:, mc * HN:(mc + 1) * HN], start=s, stop=t),
                    reads=[b_l, b_r], writes=[b_ps],
                    inc=(1 if (k == 8 or k == 16) else 0))
        p.op("sc", lambda e, oc=oc: e.copy(mo[:, oc * HN:(oc + 1) * HN], ps[:, :]),
             reads=[b_ps], writes=[b_mo])

    def wr_m0(e, slot):
        dst = bass.AP(dr["m0sh"], slot * HN, [[D, 128], [128 * D, NCH], [1, HN]])
        return e.dma_start(out=dst, in_=ap3(mo, HN, NCH, 0, HN))
    comm.send(wr_m0, [b_mo], 0)
    comm.wait_partner(0)


def emit_ffn0_phase(ctx, dr, rsrc, rdst):
    p = ctx.p
    wm0, b_wm0 = fload_weight(ctx, "p_wm0", dr["m0sh"], D, D, BF16)
    bl, b_bl = fload_vec(ctx, "p_bias", dr["bias_l"], NCH)
    ln = LN(ctx, scr=False)
    Rt = [ctx.sbuf(f"p_Rt{i}", [128, NCH * TN], F32) for i in range(2)]
    Rl = [ctx.sbuf(f"p_Rl{i}", [128, NCH * TN], F32) for i in range(2)]
    xnb = [ctx.sbuf(f"p_xn{i}", [128, NCH * TN], BF16) for i in range(2)]
    xnbufs = [[Buf(f"xn{i}_{c}") for c in range(NCH)] for i in range(2)]
    nmu = [ctx.sbuf(f"p_negmu{i}", [128, TN], F32) for i in range(2)]
    avv = [ctx.sbuf(f"p_aval{i}", [128, TN], F32) for i in range(2)]
    evb = [ctx.sbuf(f"p_ev{i}", [128, TN], F32) for i in range(2)]
    ascr = [ctx.sbuf(f"p_ascr{i}", [128, TN], F32) for i in range(2)]
    hb, b_hb = ctx.sbuf("p_hb", [128, NCH], F32)
    p.op("sc", lambda e: e.mul(hb[:, :], bl[:, :], 0.5), reads=[b_bl], writes=[b_hb])
    pools = [ctx.psum(f"p_pp{i}", [128, TN], F32) for i in range(4)]

    def fe_load(j):
        par = j % 2
        R, b_R = Rt[par]
        p.op("sy", lambda e, j=j, R=R: e.dma_start(
            out=ap3(R, TN, NCH, 0, TN), in_=r_dram(rsrc, j * TN, TN)),
            writes=[b_R], dma=True)

    def fe_rest(j):
        par = j % 2
        R, b_R = Rt[par]
        xn, b_xn = xnb[par]
        negmu, b_negmu = nmu[par]
        av, b_av = avv[par]
        scr, b_scr = ascr[par]
        ln.stats(R, b_R, TN, 0, TN, negmu, b_negmu, 0, av, b_av, 0, par=par)
        ln.apply_chunked(R, b_R, TN, 0, TN, negmu, b_negmu, 0, av, b_av, 0,
                         xn, xnbufs[par], TN, 0, scr, b_scr)

    def frontend(j):
        fe_load(j)
        fe_rest(j)

    def load_res(j):
        par = j % 2
        R, b_R = Rl[par]
        p.op("gp", lambda e, j=j, R=R: e.dma_start(
            out=ap3(R, TN, NCH, 0, TN), in_=r_dram(rsrc, j * TN, TN)),
            writes=[b_R], dma=True)

    def backend(j):
        par = j % 2
        col = j * TN
        R, b_R = Rl[par]
        xn, b_xn = xnb[par]
        ev, b_ev = evb[par]

        def evac(oc, psb, b_ps):
            p.op("sc", lambda e, oc=oc, psb=psb: e.activation(
                ev[:, :], psb[:, 0:TN], AF.Copy, scale=0.5),
                reads=[b_ps], writes=[b_ev])
            p.op("ve", lambda e, oc=oc: e.scalar_tensor_tensor(
                R[:, oc * TN:(oc + 1) * TN], ev[:, :], hb[:, oc:oc + 1],
                R[:, oc * TN:(oc + 1) * TN], ALU.add, ALU.add),
                reads=[b_ev, b_hb], writes=[b_R])

        emit_proj(ctx, wm0, b_wm0, D, xn, xnbufs[par], TN, 0, TN,
                  pools[par * 2:par * 2 + 2], evac)
        p.op("sc", lambda e, col=col, R=R: e.dma_start(
            out=r_dram(rdst, col, TN), in_=ap3(R, TN, NCH, 0, TN)),
            reads=[b_R], dma=True)

    frontend(0)
    load_res(0)
    frontend(1)
    load_res(1)
    for j in range(NT):
        if j + 2 < NT:
            fe_load(j + 2)
        backend(j)
        if j + 2 < NT:
            load_res(j + 2)
            fe_rest(j + 2)


def emit_attn0_phase(ctx, dr, rsrc, rdst):
    p = ctx.p
    wqk, b_wqk = ctx.sbuf("p_wqk", [128, NCH * 2048], BF16)
    src = bass.AP(dr["wqkvT"], 0, [[3 * D, 128], [128 * 3 * D, NCH], [1, 2048]])
    p.op("gp", lambda e, src=src: e.dma_start(out=ap3(wqk, 2048, NCH, 0, 2048), in_=src),
         writes=[b_wqk], dma=True)
    wvr, b_wvr = ctx.sbuf("p_wvr", [128, NCH * 1024], BF16)
    srcv = bass.AP(dr["wqkvT"], 2048, [[3 * D, 128], [128 * 3 * D, NCH], [1, 1024]])
    p.op("gp", lambda e, srcv=srcv: e.dma_start(out=ap3(wvr, 1024, NCH, 0, 1024), in_=srcv),
         writes=[b_wvr], dma=True)
    wout, b_wout = fload_weight(ctx, "p_wout", dr["woutT"], D, D, BF16)
    bq, b_bq = fload_vec(ctx, "p_bqkv", dr["bqkv_l"], 3 * NCH)
    bo, b_bo = fload_vec(ctx, "p_bout", dr["bout_l"], NCH)
    idt, b_idt = ctx.sbuf("p_idtb", [128, 128], BF16)
    p.op("gp", lambda e: e.dma_start(out=idt[:, :], in_=dr["ident"][:, :]),
         writes=[b_idt], dma=True)
    vbb, b_vbb = ctx.sbuf("p_vbrow", [1, 1024], BF16)
    p.op("gp", lambda e: e.dma_start(out=vbb[:, :], in_=dr["bv_row"][:, :]),
         writes=[b_vbb], dma=True)

    ln = LN(ctx, nbuf=1, scr=False)
    bqs, b_bqs = ctx.sbuf("p_bqs", [128, 3 * NCH], F32)
    p.op("sc", lambda e: e.mul(bqs[:, 0:NCH], bq[:, 0:NCH], 0.125),
         reads=[b_bq], writes=[b_bqs])
    p.op("sc", lambda e: e.copy(bqs[:, NCH:3 * NCH], bq[:, NCH:3 * NCH]),
         reads=[b_bq], writes=[b_bqs])
    Rt = [ctx.sbuf(f"p_Rt{i}", [128, NCH * TN], F32) for i in range(2)]
    xnb = [ctx.sbuf(f"p_xn{i}", [128, NCH * TN], BF16) for i in range(2)]
    nmu = [ctx.sbuf("p_negmu", [128, TN], F32)] * 2
    avv = [ctx.sbuf("p_aval", [128, TN], F32)] * 2
    ascr, b_ascr = ctx.sbuf("p_ascr", [128, TN], F32)
    qkv, _bq0 = ctx.sbuf("p_qkv", [128, 16 * TN], BF16)
    b_qkvs = [Buf(f"qkv{i}") for i in range(16)]
    vt, b_vt = ctx.sbuf("p_vt", [64, 8 * 1024], BF16)
    scor, _bs0 = ctx.sbuf("p_scor", [128, NCH * TN], F32)
    b_scors = [Buf(f"scor{i}") for i in range(NCH)]
    ebf, _be0 = ctx.sbuf("p_ebf", [128, NCH * TN], BF16)
    b_ebfs = [Buf(f"ebf{i}") for i in range(NCH)]
    mxt, b_mxt = ctx.sbuf("p_mxt", [128, 64], F32)
    smt, b_smt = ctx.sbuf("p_smt", [128, 64], F32)
    rst, b_rst = ctx.sbuf("p_rst", [128, 64], F32)
    atT = [ctx.sbuf(f"p_atT{i}", [64, 1024], BF16) for i in range(2)]
    avo, _bav0 = ctx.sbuf("p_avo", [128, NCH * TN], BF16)
    b_avos = [Buf(f"avo{i}") for i in range(NCH)]
    pool = [ctx.psum(f"p_pl{i}", [128, TN], F32) for i in range(2)]
    psS = [ctx.psum(f"p_psS{i}", [128, TN], F32) for i in range(2)]
    psT = [ctx.psum(f"p_psT{i}", [64, 1024], BF16) for i in range(2)]

    def frontend(j):
        par = j % 2
        R, b_R = Rt[par]
        xn, b_xn = xnb[par]
        negmu, b_negmu = nmu[par]
        av, b_av = avv[par]
        p.op("sy", lambda e, j=j, R=R: e.dma_start(
            out=ap3(R, TN, NCH, 0, TN), in_=r_dram(rsrc, j * TN, TN)),
            writes=[b_R], dma=True)
        ln.stats(R, b_R, TN, 0, TN, negmu, b_negmu, 0, av, b_av, 0)
        ln.apply_chunked(R, b_R, TN, 0, TN, negmu, b_negmu, 0, av, b_av, 0,
                         xn, b_xn, TN, 0, ascr, b_ascr)

    def p1_qkproj(j):
        xn, b_xn = xnb[j % 2]
        for qq in range(16):
            psb, b_psb = pool[qq % 2]
            for ic in range(NCH):
                p.op("te", lambda e, qq=qq, ic=ic, psb=psb, xn=xn: e.matmul(
                    psb[:, 0:TN], wblk(wqk, ic, qq, 2 * D),
                    bass.AP(xn, ic * TN, [[xn.ap().ap[0][0], 128], [1, TN]]),
                    start=(ic == 0), stop=(ic == NCH - 1)),
                    reads=[b_wqk, b_xn], writes=[b_psb],
                    inc=(1 if ic == NCH - 1 else 0))
            sc = 0.125 if qq < NCH else 1.0
            p.op("sc", lambda e, qq=qq, psb=psb, sc=sc: e.activation(
                qkv[:, qq * TN:(qq + 1) * TN], psb[:, 0:TN], AF.Identity,
                bias=bqs[:, qq:qq + 1], scale=sc),
                reads=[b_psb, b_bqs], writes=[b_qkvs[qq]])

    def p2_vtok(j):
        xn, b_xn = xnb[j % 2]
        for tb in range(4):
            for half in range(2):
                psb, b_psb = pool[half]
                for ic in range(NCH):
                    p.op("te", lambda e, tb=tb, half=half, ic=ic, psb=psb, xn=xn: e.matmul(
                        psb[:, 0:TN], xn[:, ic * TN + tb * 128: ic * TN + (tb + 1) * 128],
                        wvr[:, ic * 1024 + half * TN: ic * 1024 + (half + 1) * TN],
                        start=(ic == 0), stop=False),
                        reads=[b_xn, b_wvr], writes=[b_psb], inc=0)
                p.op("te", lambda e, half=half, psb=psb: e.matmul(
                    psb[:, 0:TN], ln.ones[0:1, :], vbb[0:1, half * TN:(half + 1) * TN],
                    start=False, stop=True),
                    reads=[ln.b_ones, b_vbb], writes=[b_psb])
                for sub in range(2):
                    blk = tb * 2 + sub
                    p.op("sc", lambda e, blk=blk, sub=sub, half=half, psb=psb: e.copy(
                        vt[0:64, blk * 1024 + half * TN: blk * 1024 + (half + 1) * TN],
                        psb[sub * 64:(sub + 1) * 64, 0:TN]),
                        reads=[b_psb], writes=[b_vt])

    def p3_scores(j):
        srow = scor.ap().ap[0][0]
        erow = ebf.ap().ap[0][0]
        for c in range(NCH):
            sp, b_sp = psS[c % 2]
            qc = c * TN
            kc = (NCH + c) * TN
            for blk in range(8):
                o = blk * 64
                p.op("te", lambda e, qc=qc, kc=kc, o=o, sp=sp: e.matmul(
                    sp[0:64, o:o + 64], qkv[0:64, qc + o:qc + o + 64],
                    qkv[0:64, kc + o:kc + o + 64],
                    start=True, stop=True, tile_position=(0, 0)),
                    reads=[b_qkvs[c], b_qkvs[NCH + c]], writes=[b_sp], inc=0)
                p.op("te", lambda e, qc=qc, kc=kc, o=o, sp=sp: e.matmul(
                    sp[64:128, o:o + 64], qkv[64:128, qc + o:qc + o + 64],
                    qkv[64:128, kc + o:kc + o + 64],
                    start=True, stop=True, tile_position=(64, 64)),
                    reads=[b_qkvs[c], b_qkvs[NCH + c]], writes=[b_sp],
                    inc=(1 if blk == 7 else 0))
            p.op("sc", lambda e, c=c, sp=sp: e.copy(scor[:, c * TN:(c + 1) * TN], sp[:, :]),
                 reads=[b_sp], writes=[b_scors[c]])

    def p4_softmax(j):
        s3 = scor.ap().rearrange("p (b n) -> p b n", n=64)
        e3 = ebf.ap().rearrange("p (b n) -> p b n", n=64)
        p.op("ve", lambda e: e.tensor_reduce(mxt[:, :], s3, AX.X, ALU.max, negate=True),
             reads=b_scors, writes=[b_mxt])
        p.op("ve", lambda e: e.tensor_tensor(s3, s3, bci(mxt, 0, 64), ALU.add),
             reads=b_scors + [b_mxt], writes=b_scors)
        p.op("sc", lambda e: e.activation(ebf[:, :], scor[:, :], AF.Exp),
             reads=b_scors, writes=b_ebfs)
        p.op("ve", lambda e: e.tensor_reduce(smt[:, :], e3, AX.X, ALU.add),
             reads=b_ebfs, writes=[b_smt])
        p.op("ve", lambda e: e.reciprocal(rst[:, :], smt[:, :]), reads=[b_smt], writes=[b_rst])
        p.op("ve", lambda e: e.tensor_tensor(e3, e3, bci(rst, 0, 64), ALU.mult),
             reads=b_ebfs + [b_rst], writes=b_ebfs)

    def p5_transpav(j):
        for c in range(NCH):
            at, b_at = atT[c % 2]
            pt, b_pt = psT[c % 2]
            for blk in range(8):
                p.op("te", lambda e, c=c, blk=blk, pt=pt: e.transpose(
                    pt[:, blk * 128:(blk + 1) * 128],
                    ebf[:, c * TN + blk * 64: c * TN + blk * 64 + 64], idt[:, :]),
                    reads=[b_ebfs[c], b_idt], writes=[b_pt],
                    inc=(1 if blk == 7 else 0))
            p.op("sc", lambda e, pt=pt, at=at: e.copy(at[:, :], pt[:, :]),
                 reads=[b_pt], writes=[b_at])
            ava, b_ava = psS[0]
            for blk in range(8):
                o = blk * 64
                for h in range(2):
                    vsl = vt[0:64, blk * 1024 + c * 128 + h * 64:
                             blk * 1024 + c * 128 + h * 64 + 64]
                    asl = at[0:64, blk * 128 + h * 64: blk * 128 + h * 64 + 64]
                    p.op("te", lambda e, vsl=vsl, asl=asl, o=o, h=h, ava=ava: e.matmul(
                        ava[h * 64:(h + 1) * 64, o:o + 64], vsl, asl,
                        start=True, stop=True, tile_position=(0, h * 64)),
                        reads=[b_vt, b_at], writes=[b_ava],
                        inc=(1 if (blk == 7 and h == 1) else 0))
            p.op("sc", lambda e, c=c, ava=ava: e.copy(avo[:, c * TN:(c + 1) * TN], ava[:, :]),
                 reads=[b_ava], writes=[b_avos[c]])

    def p6_out(j):
        par = j % 2
        R, b_R = Rt[par]

        def evo(oc, psb, b_ps):
            p.op("ve", lambda e, oc=oc, psb=psb: e.scalar_tensor_tensor(
                R[:, oc * TN:(oc + 1) * TN], psb[:, 0:TN], bo[:, oc:oc + 1],
                R[:, oc * TN:(oc + 1) * TN], ALU.add, ALU.add),
                reads=[b_ps, b_bo], writes=[b_R])

        emit_proj(ctx, wout, b_wout, D, avo, b_avos, TN, 0, TN, pool, evo)
        p.op("sc", lambda e, j=j, R=R: e.dma_start(
            out=r_dram(rdst, j * TN, TN), in_=ap3(R, TN, NCH, 0, TN)),
            reads=[b_R], dma=True)

    frontend(0)
    p1_qkproj(0)
    p2_vtok(0)
    frontend(1)
    for j in range(NT):
        p3_scores(j)
        p4_softmax(j)
        if j + 1 < NT:
            p1_qkproj(j + 1)
        p5_transpav(j)
        if j + 1 < NT:
            p2_vtok(j + 1)
        p6_out(j)
        if j + 2 < NT:
            frontend(j + 2)


def emit_ffn1_phase(ctx, dr, rsrc, rdst, comm, flagidx, hsh):
    p = ctx.p
    wr, b_wr = fload_weight(ctx, "p_wr1", dr["wrT1"], D, D, BF16)
    wk, b_wk = fload_weight(ctx, "p_wk1", dr["wkT1"], D, D, BF16)
    wv, b_wv = fload_weight(ctx, "p_wv1", dr["wvT1"], D, D, BF16)
    mur, b_mur = fload_vec(ctx, "p_mur", dr["mur_l"], NCH)
    muk, b_muk = fload_vec(ctx, "p_muk", dr["muk_l"], NCH)
    zm, b_zm = fload_vec(ctx, "p_zmask", dr["zmask"], NCH)
    ln = LN(ctx, scr=False)
    Rt = [ctx.sbuf(f"p_Rt{i}", [128, NCH * TN], F32) for i in range(2)]
    Rl = [ctx.sbuf("p_Rl", [128, NCH * TN], F32)]
    nmu = [ctx.sbuf(f"p_negmu{i}", [128, TN], F32) for i in range(2)]
    avv = [ctx.sbuf(f"p_aval{i}", [128, TN], F32) for i in range(2)]
    fscr, b_fscr = ctx.sbuf("p_fscr", [128, TN], F32)
    bxn, b_bxn = ctx.sbuf("p_bxn", [128, NCH], BF16)
    bmu, b_bmu = ctx.sbuf("p_bmu", [128, 1], F32)
    bav, b_bav = ctx.sbuf("p_bav", [128, 1], F32)
    bt, b_bt = ctx.sbuf("p_bt", [128, NCH], F32)
    hrecv, b_hrecv = ctx.sbuf("p_hrecv", [128, NCH], BF16)
    carry, b_carry = ctx.sbuf("p_carry", [128, NCH], BF16)
    XW = TN + 1
    xeb = [ctx.sbuf(f"p_xe{i}", [128, NCH * XW], BF16) for i in range(2)]
    dt, b_dt = ctx.sbuf("p_dt", [128, TN], BF16)
    xrb = [ctx.sbuf(f"p_xr{i}", [128, NCH * TN], BF16) for i in range(2)]
    xkb = [ctx.sbuf(f"p_xk{i}", [128, NCH * TN], BF16) for i in range(2)]
    rg, b_rg = ctx.sbuf("p_rg", [128, NCH * TN], BF16)
    kact, b_kact = ctx.sbuf("p_kact", [128, NCH * TN], BF16)
    tv, b_tv = fscr, b_fscr
    krel, b_krel = tv, b_tv
    pools = [ctx.psum(f"p_pp{i}", [128, TN], F32) for i in range(4)]

    p.op("sy", lambda e: e.dma_start(out=ap3(bt, 1, NCH, 0, 1),
                                     in_=r_dram(rsrc, T - 1, 1),
                                     allow_slow_non_contiguous=True),
         writes=[b_bt], dma=True)
    ln.stats(bt, b_bt, 1, 0, 1, bmu, b_bmu, 0, bav, b_bav, 0)
    ln.apply_chunked(bt, b_bt, 1, 0, 1, bmu, b_bmu, 0, bav, b_bav, 0,
                     bxn, b_bxn, 1, 0, fscr, b_fscr)

    def wr_h(e, slot):
        return e.dma_start(out=hsh[slot * 128:(slot + 1) * 128, :], in_=bxn[:, :])
    comm.send(wr_h, [b_bxn], flagidx)
    ptok = comm.wait_partner(flagidx)
    p.op("gp", lambda e: e.dma_start(out=hrecv[:, :], in_=hsh[0:128, :]),
         writes=[b_hrecv], deps=[ptok], dma=True)
    p.op("ve", lambda e: e.tensor_tensor(carry[:, :], hrecv[:, :], zm[:, :], ALU.mult),
         reads=[b_hrecv, b_zm], writes=[b_carry])

    def fe_load(j):
        par = j % 2
        R, b_R = Rt[par]
        p.op("sy", lambda e, j=j, R=R: e.dma_start(
            out=ap3(R, TN, NCH, 0, TN), in_=r_dram(rsrc, j * TN, TN)),
            writes=[b_R], dma=True)

    def fe_rest(j):
        par = j % 2
        col = j * TN
        R, b_R = Rt[par]
        negmu, b_negmu = nmu[par]
        avl, b_avl = avv[par]
        xe, b_xe = xeb[par]
        xr, b_xr = xrb[par]
        xk, b_xk = xkb[par]
        ln.stats(R, b_R, TN, 0, TN, negmu, b_negmu, 0, avl, b_avl, 0, par=par)
        p.op("ve", lambda e, xe=xe: e.tensor_copy(ap3(xe, XW, NCH, 0, 1),
                                                  ap3(carry, 1, NCH, 0, 1)),
             reads=[b_carry], writes=[b_xe])
        ln.apply_chunked(R, b_R, TN, 0, TN, negmu, b_negmu, 0, avl, b_avl, 0,
                         xe, b_xe, XW, 1, fscr, b_fscr)
        p.op("ve", lambda e, xe=xe: e.tensor_copy(ap3(carry, 1, NCH, 0, 1),
                                                  ap3(xe, XW, NCH, TN, 1)),
             reads=[b_xe], writes=[b_carry])
        xrow = xe.ap().ap[0][0]
        for ic in range(NCH):
            cur = bass.AP(xe, ic * XW + 1, [[xrow, 128], [1, TN]])
            prv = bass.AP(xe, ic * XW, [[xrow, 128], [1, TN]])
            p.op("ve", lambda e, cur=cur, prv=prv: e.tensor_tensor(
                dt[:, :], cur, prv, ALU.subtract),
                reads=[b_xe], writes=[b_dt])
            for (mu, b_mu, xdst, b_xdst) in ((mur, b_mur, xr, b_xr), (muk, b_muk, xk, b_xk)):
                p.op("ve", lambda e, ic=ic, mu=mu, xdst=xdst, prv=prv: e.scalar_tensor_tensor(
                    xdst[:, ic * TN:(ic + 1) * TN], dt[:, :],
                    mu[:, ic:ic + 1], prv,
                    ALU.mult, ALU.add),
                    reads=[b_dt, b_mu, b_xe], writes=[b_xdst])

    def load_res(j):
        R, b_R = Rl[0]
        p.op("gp", lambda e, j=j, R=R: e.dma_start(
            out=ap3(R, TN, NCH, 0, TN), in_=r_dram(rsrc, j * TN, TN)),
            writes=[b_R], dma=True)

    def backend(j):
        par = j % 2
        col = j * TN
        R, b_R = Rl[0]
        xr, b_xr = xrb[par]
        xk, b_xk = xkb[par]

        def ev_r(oc, psb, b_ps):
            p.op("sc", lambda e, oc=oc, psb=psb: e.activation(
                rg[:, oc * TN:(oc + 1) * TN], psb[:, 0:TN], AF.Sigmoid),
                reads=[b_ps], writes=[b_rg])

        emit_proj(ctx, wr, b_wr, D, xr, b_xr, TN, 0, TN, pools[0:2], ev_r)

        def ev_k(oc, psb, b_ps):
            p.op("sc", lambda e, psb=psb: e.activation(
                krel[:, :], psb[:, 0:TN], AF.Relu),
                reads=[b_ps], writes=[b_krel])
            p.op("sc", lambda e, oc=oc: e.activation(
                kact[:, oc * TN:(oc + 1) * TN], krel[:, :], AF.Square),
                reads=[b_krel], writes=[b_kact])

        emit_proj(ctx, wk, b_wk, D, xk, b_xk, TN, 0, TN, pools[2:4], ev_k)

        def ev_v(oc, psb, b_ps, R=R, b_R=b_R):
            p.op("ve", lambda e, oc=oc, psb=psb: e.tensor_tensor(
                tv[:, :], psb[:, 0:TN], rg[:, oc * TN:(oc + 1) * TN], ALU.mult),
                reads=[b_ps, b_rg], writes=[b_tv])
            p.op("ve", lambda e, oc=oc: e.scalar_tensor_tensor(
                R[:, oc * TN:(oc + 1) * TN], tv[:, :], 0.5,
                R[:, oc * TN:(oc + 1) * TN], ALU.mult, ALU.add),
                reads=[b_tv], writes=[b_R])

        emit_proj(ctx, wv, b_wv, D, kact, b_kact, TN, 0, TN, pools[0:2], ev_v)
        p.op("sc", lambda e, col=col, R=R: e.dma_start(
            out=r_dram(rdst, col, TN), in_=ap3(R, TN, NCH, 0, TN)),
            reads=[b_R], dma=True)

    def frontend(j):
        fe_load(j)
        fe_rest(j)

    frontend(0)
    load_res(0)
    frontend(1)
    for j in range(NT):
        if j + 2 < NT:
            fe_load(j + 2)
        backend(j)
        if j + 1 < NT:
            load_res(j + 1)
        if j + 2 < NT:
            fe_rest(j + 2)


def emit_lin_phase(ctx, dr, rsrc, rdst, comm):
    p = ctx.p
    wk, b_wk = fload_weight(ctx, "p_wkla", dr["wkT"], D, 128, BF16)
    wq, b_wq = fload_weight(ctx, "p_wqla", dr["wqT"], D, 128, BF16)
    idt, b_idt = fload_weight(ctx, "p_ident2", dr["ident"], 128, 128, BF16)
    gam, b_gam = fload_vec(ctx, "p_gam", dr["gam_l"], NCH)
    bet, b_bet = fload_vec(ctx, "p_bet", dr["bet_l"], NCH)
    ln = LN(ctx, nbuf=2)
    Rt = [ctx.sbuf(f"p_Rt{i}", [128, NCH * TN], F32) for i in range(2)]
    xnb = [ctx.sbuf(f"p_xn{i}", [128, NCH * TN], BF16) for i in range(2)]
    nmu = [ctx.sbuf(f"p_negmu{i}", [128, TN], F32) for i in range(2)]
    avv = [ctx.sbuf(f"p_aval{i}", [128, TN], F32) for i in range(2)]
    lscr, b_lscr = ctx.sbuf("p_lscr", [128, TN], F32)
    ktok, b_ktok = ctx.sbuf("p_ktok", [128, 128], BF16)
    jtok, b_jtok = ctx.sbuf("p_jtok", [128, 1024], BF16)
    msb, b_msb = ctx.sbuf("p_msb", [128, 1024], F32)
    m0r, b_m0r = ctx.sbuf("p_m0r", [128, 1024], F32)
    m1sb, b_m1sb = ctx.sbuf("p_m1sb", [128, 1024], F32)
    mbf, b_mbf = ctx.sbuf("p_mbf", [128, 1024], BF16)
    qbf, b_qbf = ctx.sbuf("p_qbf", [128, TN], BF16)
    jn, b_jn = ctx.sbuf("p_jn", [128, NCH * TN], F32)
    nmu2 = [ctx.sbuf(f"p_nmu2{i}", [128, TN], F32) for i in range(2)]
    av2 = [ctx.sbuf(f"p_av2{i}", [128, TN], F32) for i in range(2)]
    g1, b_g1 = ctx.sbuf("p_g1", [128, TN], F32)
    psK, b_psK = ctx.psum("p_psK", [128, 128], F32)
    psJ, b_psJ = ctx.psum("p_psJ", [128, 1024], BF16)
    psM = [ctx.psum(f"p_psM{i}", [128, TN], F32) for i in range(2)]
    pools = psM

    nblk = T // 128

    def m_load(j):
        par = j % 2
        R, b_R = Rt[par]
        p.op("sy", lambda e, j=j, R=R: e.dma_start(
            out=ap3(R, TN, NCH, 0, TN), in_=r_dram(rsrc, j * TN, TN)),
            writes=[b_R], dma=True)

    def m_rest(j):
        par = j % 2
        col = j * TN
        R, b_R = Rt[par]
        xn, b_xn = xnb[par]
        negmu, b_negmu = nmu[par]
        avl, b_avl = avv[par]
        ln.stats(R, b_R, TN, 0, TN, negmu, b_negmu, 0, avl, b_avl, 0, par=par)
        ln.apply_bf(par, TN, negmu, b_negmu, 0, avl, b_avl, 0, xn, b_xn, TN, 0)
        p.op("sc", lambda e, col=col, xn=xn: e.dma_start(
            out=bass.AP(dr["xstash"], col, [[T, 128], [128 * T, NCH], [1, TN]]),
            in_=ap3(xn, TN, NCH, 0, TN)),
            reads=[b_xn], dma=True)

    def m_frontend(j):
        m_load(j)
        m_rest(j)

    def m_backend(j):
        par = j % 2
        xn, b_xn = xnb[par]
        for tb in range(TN // 128):
            gi = j * (TN // 128) + tb
            for ic in range(NCH):
                p.op("te", lambda e, ic=ic, tb=tb, xn=xn: e.matmul(
                    psK[:, :], xn[:, ic * TN + tb * 128: ic * TN + (tb + 1) * 128],
                    wk[:, ic * 128:(ic + 1) * 128],
                    start=(ic == 0), stop=(ic == NCH - 1)),
                    reads=[b_xn, b_wk], writes=[b_psK],
                    inc=(1 if ic == NCH - 1 else 0))
            p.op("sc", lambda e: e.copy(ktok[:, :], psK[:, :]), reads=[b_psK], writes=[b_ktok])
            for ic in range(NCH):
                p.op("te", lambda e, ic=ic, tb=tb, xn=xn: e.transpose(
                    psJ[:, ic * 128:(ic + 1) * 128],
                    xn[:, ic * TN + tb * 128: ic * TN + (tb + 1) * 128], idt[:, :]),
                    reads=[b_xn, b_idt], writes=[b_psJ],
                    inc=(1 if ic == NCH - 1 else 0))
            p.op("sc", lambda e: e.copy(jtok[:, :], psJ[:, :]), reads=[b_psJ], writes=[b_jtok])
            for half in range(2):
                psb, b_psb = psM[half]
                p.op("te", lambda e, psb=psb, half=half, s=(gi == 0), t=(gi == nblk - 1): e.matmul(
                    psb[:, :], ktok[:, :], jtok[:, half * TN:(half + 1) * TN],
                    start=s, stop=t),
                    reads=[b_ktok, b_jtok], writes=[b_psb],
                    inc=(1 if gi == nblk - 1 else 0))

    m_frontend(0)
    m_frontend(1)
    for j in range(NT):
        if j + 2 < NT:
            m_load(j + 2)
        m_backend(j)
        if j + 2 < NT:
            m_rest(j + 2)

    for half in range(2):
        psb, b_psb = psM[half]
        p.op("sc", lambda e, half=half, psb=psb: e.copy(
            msb[:, half * TN:(half + 1) * TN], psb[:, :]),
            reads=[b_psb], writes=[b_msb])

    def wr_m(e, slot):
        return e.dma_start(out=dr["msh"][slot * 128:(slot + 1) * 128, :], in_=msb[:, :])
    comm.send(wr_m, [b_msb], 1)
    ptok = comm.wait_partner(1)
    p.op("gp", lambda e: e.dma_start(out=m0r[:, :], in_=dr["msh"][0:128, :]),
         writes=[b_m0r], deps=[ptok], dma=True)
    p.op("gp", lambda e: e.dma_start(out=m1sb[:, :], in_=dr["msh"][128:256, :]),
         writes=[b_m1sb], deps=[ptok], dma=True)
    p.op("ve", lambda e: e.tensor_tensor(mbf[:, :], m0r[:, :], m1sb[:, :], ALU.add),
         reads=[b_m0r, b_m1sb], writes=[b_mbf])

    def a_frontend(j):
        par = j % 2
        col = j * TN
        R, b_R = Rt[par]
        xn, b_xn = xnb[par]
        p.op("sy", lambda e, col=col, R=R: e.dma_start(
            out=ap3(R, TN, NCH, 0, TN), in_=r_dram(rsrc, col, TN)),
            writes=[b_R], dma=True)
        p.op("sy", lambda e, col=col, xn=xn: e.dma_start(
            out=ap3(xn, TN, NCH, 0, TN),
            in_=bass.AP(dr["xstash"], col, [[T, 128], [128 * T, NCH], [1, TN]])),
            writes=[b_xn], dma=True)

    def a_backend(j):
        par = j % 2
        col = j * TN
        R, b_R = Rt[par]
        xn, b_xn = xnb[par]
        negmu2, b_negmu2 = nmu2[par]
        avl2, b_avl2 = av2[par]

        def evq(oc, psb, b_ps):
            p.op("sc", lambda e, psb=psb: e.copy(qbf[:, :], psb[:, 0:TN]),
                 reads=[b_ps], writes=[b_qbf])

        emit_proj(ctx, wq, b_wq, 128, xn, b_xn, TN, 0, TN, pools, evq)
        for dc in range(NCH):
            psb, b_psb = pools[dc % 2]
            p.op("te", lambda e, dc=dc, psb=psb: e.matmul(
                psb[:, 0:TN], mbf[:, dc * 128:(dc + 1) * 128], qbf[:, :],
                start=True, stop=True),
                reads=[b_mbf, b_qbf], writes=[b_psb])
            p.op("ve", lambda e, dc=dc, psb=psb, xn=xn: e.scalar_tensor_tensor(
                jn[:, dc * TN:(dc + 1) * TN], psb[:, 0:TN], RSQ_DK,
                xn[:, dc * TN:(dc + 1) * TN], ALU.mult, ALU.add),
                reads=[b_psb, b_xn], writes=[b_jn])
        ln.stats(jn, b_jn, TN, 0, TN, negmu2, b_negmu2, 0, avl2, b_avl2, 0, par=par)
        scr, b_scr = ln.scr[par]
        p.op("ve", lambda e, scr=scr: e.tensor_tensor(
            ap3(scr, TN, NCH, 0, TN), ap3(jn, TN, NCH, 0, TN),
            bc3(negmu2, 0, TN, NCH), ALU.add),
            reads=[b_jn, b_negmu2], writes=[b_scr])
        p.op("ve", lambda e, scr=scr: e.tensor_tensor(
            ap3(scr, TN, NCH, 0, TN), ap3(scr, TN, NCH, 0, TN),
            bc3(avl2, 0, TN, NCH), ALU.mult),
            reads=[b_avl2], writes=[b_scr])
        for ic in range(NCH):
            p.op("ve", lambda e, ic=ic, scr=scr: e.tensor_scalar(
                g1[:, :], scr[:, ic * TN:(ic + 1) * TN],
                gam[:, ic:ic + 1], bet[:, ic:ic + 1], ALU.mult, ALU.add),
                reads=[b_scr, b_gam, b_bet], writes=[b_g1])
            p.op("ve", lambda e, ic=ic, R=R: e.tensor_tensor(
                R[:, ic * TN:(ic + 1) * TN], R[:, ic * TN:(ic + 1) * TN],
                g1[:, :], ALU.add),
                reads=[b_g1], writes=[b_R])
        p.op("sc", lambda e, col=col, R=R: e.dma_start(
            out=r_dram(rdst, col, TN), in_=ap3(R, TN, NCH, 0, TN)),
            reads=[b_R], dma=True)

    a_frontend(0)
    for j in range(NT):
        if j + 1 < NT:
            a_frontend(j + 1)
        a_backend(j)


def build_fused():
    nc = bass.Bass("TRN2", target_bir_lowering=False)
    with contextlib.ExitStack() as st0:
        ctx = Ctx(nc, st0)
        p = ctx.p
        dr = {}
        rin = nc.dram_tensor("r", [D, T], F32, kind="ExternalInput")
        rout = nc.dram_tensor("rout", [D, T], F32, kind="ExternalOutput")
        for key, shape in (("wrT", [D, D]), ("wiT", [D, D]), ("fr", [D, D]), ("nfi", [D, D]),
                           ("grh", [D, 512]), ("ngih", [D, 512]), ("gih", [D, 512]),
                           ("bias_l", [128, NCH]),
                           ("wqkvT", [D, 3 * D]), ("bqkv_l", [128, 3 * NCH]),
                           ("woutT", [D, D]), ("bout_l", [128, NCH]),
                           ("ident", [128, 128]),
                           ("wrT1", [D, D]), ("wkT1", [D, D]), ("wvT1", [D, D]),
                           ("mur_l", [128, NCH]), ("muk_l", [128, NCH]),
                           ("zmask", [128, NCH]),
                           ("wqT", [D, 128]), ("wkT", [D, 128]),
                           ("gam_l", [128, NCH]), ("bet_l", [128, NCH]),
                           ("bv_row", [1, 1024])):
            dr[key] = nc.dram_tensor(key, shape, F32, kind="ExternalInput")
        dr["m0sh"] = nc.dram_tensor("m0sh", [D, D], F32, addr_space="Shared")
        dr["msh"] = nc.dram_tensor("msh", [256, D], F32, addr_space="Shared")
        dr["hsh1"] = nc.dram_tensor("hsh1", [256, NCH], BF16, addr_space="Shared")
        dr["hsh2"] = nc.dram_tensor("hsh2", [256, NCH], BF16, addr_space="Shared")
        dr["xstash"] = nc.dram_tensor("xstash", [D, T], BF16)
        ra = nc.dram_tensor("ra", [D, T], F32)
        rb = nc.dram_tensor("rb", [D, T], F32)

        comm = PairComm(ctx, st0)
        sems = {name: st0.enter_context(nc.semaphore(name))
                for name in ("te", "sc", "ve", "gp", "dsy", "dsc", "dgp", "dve")}

        def phase(fn, *args):
            with contextlib.ExitStack() as pst:
                old = ctx.stack
                ctx.stack = pst
                fn(ctx, dr, *args)
                p.phase_barrier()
                p.flush(nc, sems)
                ctx.stack = old

        phase(emit_m0_phase, comm)
        phase(emit_ffn0_phase, rin, rb)
        phase(emit_attn0_phase, rb, ra)
        phase(emit_ffn0_phase, ra, rb)
        phase(emit_ffn1_phase, rb, ra, comm, 2, dr["hsh1"])
        phase(emit_lin_phase, ra, rb, comm)
        phase(emit_ffn1_phase, rb, rout, comm, 3, dr["hsh2"])

        p.finish()
        p.flush(nc, sems)
    return nc


def kernel_fused(L0_attn_x, L0_attn_W_qkv, L0_attn_b_qkv, L0_attn_W_out, L0_attn_b_out,
                 L0_ffn_w_freq_real, L0_ffn_w_freq_imag, L0_ffn_bias,
                 L1_attn_W_Q, L1_attn_W_K, L1_attn_gamma, L1_attn_beta,
                 L1_ffn_W_r, L1_ffn_W_k, L1_ffn_W_v, L1_ffn_mu_r, L1_ffn_mu_k):
    cores = list(range(8))
    x = np.asarray(L0_attn_x, np.float32)
    Fr, Fi, Gr, Gi = _dft_constants()
    nfi = np.ascontiguousarray(-Fi)
    wrT = np.ascontiguousarray(np.asarray(L0_ffn_w_freq_real, np.float32).T)
    wiT = np.ascontiguousarray(np.asarray(L0_ffn_w_freq_imag, np.float32).T)
    wqkvT = np.ascontiguousarray(np.asarray(L0_attn_W_qkv, np.float32).T)
    woutT = np.ascontiguousarray(np.asarray(L0_attn_W_out, np.float32).T)
    wrT1 = np.ascontiguousarray(np.asarray(L1_ffn_W_r, np.float32).T)
    wkT1 = np.ascontiguousarray(np.asarray(L1_ffn_W_k, np.float32).T)
    wvT1 = np.ascontiguousarray(np.asarray(L1_ffn_W_v, np.float32).T)
    wqT = np.ascontiguousarray(np.asarray(L1_attn_W_Q, np.float32).T)
    wkT = np.ascontiguousarray(np.asarray(L1_attn_W_K, np.float32).T)
    ident = np.eye(128, dtype=np.float32)
    common = {
        "wrT": wrT, "wiT": wiT, "fr": Fr, "nfi": nfi,
        "bias_l": _vec_l(np.asarray(L0_ffn_bias, np.float32)),
        "wqkvT": wqkvT, "bqkv_l": _vec_l(np.asarray(L0_attn_b_qkv, np.float32)),
        "woutT": woutT, "bout_l": _vec_l(np.asarray(L0_attn_b_out, np.float32)),
        "bv_row": np.ascontiguousarray(np.asarray(L0_attn_b_qkv, np.float32)[2048:3072].reshape(1, 1024)),
        "ident": ident,
        "wrT1": wrT1, "wkT1": wkT1, "wvT1": wvT1,
        "mur_l": _vec_l(np.asarray(L1_ffn_mu_r, np.float32)),
        "muk_l": _vec_l(np.asarray(L1_ffn_mu_k, np.float32)),
        "wqT": wqT, "wkT": wkT,
        "gam_l": _vec_l(np.asarray(L1_attn_gamma, np.float32)),
        "bet_l": _vec_l(np.asarray(L1_attn_beta, np.float32)),
    }
    nc = build_fused()
    in_maps = []
    for c in cores:
        b, h = c // 2, c % 2
        sl = slice(h * 512, (h + 1) * 512)
        m = dict(common)
        m["r"] = np.ascontiguousarray(x[b, h * T:(h + 1) * T, :].T)
        m["grh"] = np.ascontiguousarray(Gr[:, sl])
        m["ngih"] = np.ascontiguousarray(-Gi[:, sl])
        m["gih"] = np.ascontiguousarray(Gi[:, sl])
        m["cid"] = np.array([[h]], np.int32)
        m["zmask"] = np.zeros((128, NCH), np.float32) if h == 0 else np.ones((128, NCH), np.float32)
        in_maps.append(m)
    res = run_bass_kernel_spmd(nc, in_maps, core_ids=cores)
    out = np.empty_like(x)
    for c in cores:
        b, h = c // 2, c % 2
        out[b, h * T:(h + 1) * T, :] = res.results[c]["rout"].T
    return out
